# version 20
# baseline (speedup 1.0000x reference)
import math
import numpy as np
import ml_dtypes
import concourse.bass as bass
import concourse.mybir as mybir
from concourse.bass_utils import run_bass_kernel_spmd

F32 = mybir.dt.float32
BF16 = mybir.dt.bfloat16
AF = mybir.ActivationFunctionType
AL = mybir.AluOpType
AX = mybir.AxisListType.X
bf16 = ml_dtypes.bfloat16

QUEUES = ("pe", "act", "dve", "pool", "syd", "gpd", "scd")
ENGINE_OF = {"pe": "tensor", "act": "scalar", "dve": "vector", "pool": "gpsimd",
             "syd": "sync", "gpd": "gpsimd", "scd": "scalar"}
DMA_QUEUES = {"syd", "gpd", "scd"}


class Op:
    __slots__ = ("queue", "ch", "emit", "reads", "writes", "deps", "need_inc",
                 "inc_idx", "waits", "tag")

    def __init__(self, queue, ch, emit, reads, writes, tag):
        self.queue = queue
        self.ch = ch
        self.emit = emit
        self.reads = reads
        self.writes = writes
        self.deps = set()
        self.need_inc = False
        self.inc_idx = None
        self.waits = ()
        self.tag = tag


class Prog:

    def __init__(self, nc):
        self.nc = nc
        self.ops = []
        self.key_w = {}
        self.key_r = {}
        self.last_op = {}
        self.nsem = 0

    def op(self, queue, emit, reads=(), writes=(), tag="", ch=None):
        if ch is None:
            if queue in DMA_QUEUES:
                ch = writes[0] if writes else reads[0]
            else:
                ch = queue
        o = Op(queue, ch, emit, tuple(reads), tuple(writes), tag)
        for k in o.reads:
            for w in self.key_w.get(k, ()):
                o.deps.add(w)
        for k in o.writes:
            for w in self.key_w.get(k, ()):
                o.deps.add(w)
            for r in self.key_r.get(k, ()):
                o.deps.add(r)
        for k in o.writes:
            self.key_w[k] = [o]
            self.key_r[k] = []
        for k in o.reads:
            self.key_r.setdefault(k, []).append(o)
        o.deps.discard(o)
        self.ops.append(o)
        self.last_op[o.ch] = o
        return o

    def barrier(self):
        lasts = [op for op in self.last_op.values() if op is not None]
        for q in QUEUES:
            o = Op(q, q, (lambda e: e.nop()), (), (), "barrier")
            o.deps.update(lasts)
            self.ops.append(o)
            self.last_op[q] = o
        self.key_w.clear()
        self.key_r.clear()

    @staticmethod
    def _skip(d, o):
        return d.queue == "pe" and o.queue == "pe"

    def materialize(self, block):
        for o in self.ops:
            if o.queue in DMA_QUEUES or o.queue == "pool":
                o.need_inc = True
            for d in o.deps:
                if not self._skip(d, o):
                    d.need_inc = True
        cnt = {}
        for o in self.ops:
            if o.need_inc:
                cnt[o.ch] = cnt.get(o.ch, 0) + 1
                o.inc_idx = cnt[o.ch]
        sems = {c: self.nc.alloc_semaphore(f"s{i}")
                for i, c in enumerate(sorted(cnt))}
        assert len(sems) <= 100, f"too many sem channels: {len(sems)}"
        dma_ch = {o.ch for o in self.ops if o.queue in DMA_QUEUES}
        clock = {}
        snap = {}
        for o in self.ops:
            oc = clock.setdefault(o.ch, {})
            needed = {}
            for d in o.deps:
                if self._skip(d, o):
                    continue
                assert d.inc_idx is not None, (d.tag, o.tag)
                if oc.get(d.ch, 0) < d.inc_idx:
                    needed[d.ch] = max(needed.get(d.ch, 0), d.inc_idx)
            o.waits = tuple(sorted(needed.items()))
            for pq, v in o.waits:
                for k2, v2 in snap[(pq, v)].items():
                    if oc.get(k2, 0) < v2:
                        oc[k2] = v2
                if oc.get(pq, 0) < v:
                    oc[pq] = v
            if o.need_inc:
                sn = dict(oc)
                sn[o.ch] = o.inc_idx
                snap[(o.ch, o.inc_idx)] = sn
        by_engine = {}
        for o in self.ops:
            by_engine.setdefault(ENGINE_OF[o.queue], []).append(o)
        for engine, ops in by_engine.items():
            def body(e, ops=ops):
                for o in ops:
                    for pq, v in o.waits:
                        e.wait_ge(sems[pq], v * (16 if pq in dma_ch else 1))
                    inst = o.emit(e)
                    if o.need_inc:
                        inst.then_inc(sems[o.ch],
                                      16 if o.ch in dma_ch else 1)
            getattr(block, engine)(body)


class SbufAlloc:
    def __init__(self, nc):
        self.nc = nc
        self.off = 16640
        self.n = 0

    def alloc(self, name, shape, dtype):
        t = self.nc.alloc_sbuf_tensor_at(f"{name}_{self.n}", list(shape), dtype,
                                         offset=self.off)
        self.n += 1
        per_part = int(np.prod(shape[1:])) * mybir.dt.size(dtype)
        self.off += (per_part + 31) & ~31
        assert self.off <= 229376 - 64, f"SBUF overflow at {name}: {self.off}"
        return t

    def mark(self):
        return self.off

    def reset(self, m):
        self.off = m


CFG_FULL = dict(D=4096, SEQ=4096, OWN=2048, HEADS=32, KVH=8)
CFG_SMALL = dict(D=512, SEQ=1024, OWN=512, HEADS=4, KVH=1)


def build_program(cfg):
    D = cfg["D"]; SEQ = cfg["SEQ"]; OWN = cfg["OWN"]
    HEADS = cfg["HEADS"]; KVH = cfg["KVH"]
    HD = 128
    NEXP, LORA = 4, 16
    DC = D // 128
    ER = NEXP * LORA
    ERL = ER + NEXP
    GF = SEQ // 512
    GO = OWN // 512
    QPK = HEADS // KVH
    KC = SEQ // 128
    NQUAD = (KC + 3) // 4
    DKV = KVH * HD
    JCN = D // 512
    SCL = 1.0 / math.sqrt(HD)
    NSG = max(1, SEQ // 2048)
    SGW = SEQ // NSG
    TCN = SGW // 512
    QC = OWN // 512

    nc = bass.Bass("TRN2", target_bir_lowering=False)

    x0 = nc.dram_tensor("x0", [OWN, D], F32, kind="ExternalInput").ap()
    wq = nc.dram_tensor("wq", [D, D], F32, kind="ExternalInput").ap()
    wk = nc.dram_tensor("wk", [D, DKV], F32, kind="ExternalInput").ap()
    wv = nc.dram_tensor("wv", [D, DKV], F32, kind="ExternalInput").ap()
    wo = nc.dram_tensor("wo", [D, D], F32, kind="ExternalInput").ap()
    awr = nc.dram_tensor("awr", [D, ERL], F32, kind="ExternalInput").ap()
    bfl = nc.dram_tensor("bfl", [ER, D], F32, kind="ExternalInput").ap()
    out = nc.dram_tensor("out", [OWN, D], F32, kind="ExternalOutput").ap()

    h1h = nc.dram_tensor("h1h", [OWN, D], BF16, kind="Internal").ap()
    h2h = nc.dram_tensor("h2h", [OWN, D], BF16, kind="Internal").ap()
    h3h = nc.dram_tensor("h3h", [OWN, D], BF16, kind="Internal").ap()
    x1h = nc.dram_tensor("x1h", [OWN, D], F32, kind="Internal").ap()
    x2h = nc.dram_tensor("x2h", [OWN, D], F32, kind="Internal").ap()
    qth = nc.dram_tensor("qth", [HEADS, 128, OWN], BF16, kind="Internal").ap()
    kin = nc.dram_tensor("kin", [KVH * 128, OWN], BF16, kind="Internal").ap()
    vin = nc.dram_tensor("vin", [KVH * 128, OWN], BF16, kind="Internal").ap()
    kout = nc.dram_tensor("kout", [2 * KVH * 128, OWN], BF16, kind="Internal").ap()
    vout = nc.dram_tensor("vout", [2 * KVH * 128, OWN], BF16, kind="Internal").ap()
    oh = nc.dram_tensor("oh", [QC, HEADS, 128, 512], BF16, kind="Internal").ap()

    onesc = nc.inline_tensor(np.ones((128, 128), dtype=bf16), name="c_ones").ap()
    idbc = nc.inline_tensor(np.eye(128, dtype=bf16), name="c_idb").ap()
    idfc = nc.inline_tensor(np.eye(128, dtype=np.float32), name="c_idf").ap()
    E4np = np.zeros((NEXP, ER), dtype=bf16)
    for e_ in range(NEXP):
        E4np[e_, e_ * LORA:(e_ + 1) * LORA] = 1
    e4c = nc.inline_tensor(E4np, name="c_e4").ap()

    PQ = nc.alloc_psum_tensor("PQ", [128, 2048], F32)
    P4 = nc.alloc_psum_tensor("P4", [128, 512], F32)
    P5 = nc.alloc_psum_tensor("P5", [128, 512], F32)
    P6 = nc.alloc_psum_tensor("P6", [128, 512], F32)
    PB = nc.alloc_psum_tensor("PB", [128, 1024], BF16)

    p = Prog(nc)
    sb = SbufAlloc(nc)

    CZ = sb.alloc("cz", [128, 4], F32)
    CZB = sb.alloc("czb", [128, 4], BF16)
    p.op("dve", lambda e: e.memset(CZ[:, 0:1], 0.0), writes=["cz"], tag="cz0")
    p.op("dve", lambda e: e.memset(CZ[:, 1:2], 1.0), writes=["cz"], tag="cz1")
    p.op("dve", lambda e: e.memset(CZB[:, 0:1], 0.0), writes=["cz"], tag="czb0")
    p.op("dve", lambda e: e.memset(CZB[:, 1:2], 1.0), writes=["cz"], tag="czb1")
    nc.const_aps.register(F32, 0.0, CZ[:, 0:1])
    nc.const_aps.register(F32, 1.0, CZ[:, 1:2])
    nc.const_aps.register(BF16, 0.0, CZB[:, 0:1])
    nc.const_aps.register(BF16, 1.0, CZB[:, 1:2])

    ones_sb = sb.alloc("ones", [128, 128], BF16)
    idb_sb = sb.alloc("idb", [128, 128], BF16)
    idf_sb = sb.alloc("idf", [128, 128], F32)
    e4_sb = sb.alloc("e4", [NEXP, ER], BF16)
    atw_sb = sb.alloc("atw", [128, DC * ERL], BF16)
    bb_sb = sb.alloc("bb", [ER, D], BF16)

    ph0 = sb.mark()
    atwf = sb.alloc("atwf", [128, DC * ERL], F32)
    bf_f = sb.alloc("bf_f", [ER, D], F32)
    p.op("syd", lambda e: e.dma_start(ones_sb[:], onesc), writes=["ones"], tag="ld_ones")
    p.op("syd", lambda e: e.dma_start(idb_sb[:], idbc), writes=["idb"], tag="ld_idb")
    p.op("syd", lambda e: e.dma_start(idf_sb[:], idfc), writes=["idf"], tag="ld_idf")
    p.op("syd", lambda e: e.dma_start(e4_sb[:], e4c), writes=["e4"], tag="ld_e4")
    p.op("syd", lambda e: e.dma_start(
        atwf.rearrange("p (dc q) -> p dc q", q=ERL),
        awr.rearrange("(dc p) q -> p dc q", p=128)), writes=["atwf"], tag="ld_awr")
    p.op("syd", lambda e: e.dma_start(bf_f[:], bfl), writes=["bf_f"], tag="ld_bf")
    p.op("dve", lambda e: e.tensor_copy(atw_sb[:], atwf[:]),
         reads=["atwf"], writes=["atw"], tag="cast_atw")
    p.op("act", lambda e: e.mul(bb_sb[:], bf_f[:], 2.0),
         reads=["bf_f"], writes=["bb"], tag="cast_bb")
    p.barrier()
    sb.reset(ph0)

    def emit_ln_cast(X, XCS, TRASH, ST, xk, stk, hrow, hkey, tag, xcpfx):
        for st_ in range(4):
            p.op("dve", lambda e, st_=st_: e.reduce_sum(ST[:, st_:st_ + 1], X[:, st_, :], axis=AX),
                 reads=[f"{xk}:{st_}"], writes=[f"{stk}:a"], tag=f"{tag}_sum{st_}")
            p.op("act", lambda e, st_=st_: e.activation(TRASH[:], X[:, st_, :], AF.Square,
                                                        accum_out=ST[:, 4 + st_:5 + st_]),
                 reads=[f"{xk}:{st_}"], writes=["lntrash", f"{stk}:b"], tag=f"{tag}_sq{st_}")
        p.op("dve", lambda e: e.tensor_scalar(ST[:, 8:12], ST[:, 0:4], 1.0 / D, None, AL.mult),
             reads=[f"{stk}:a"], writes=[f"{stk}:m"], tag=f"{tag}_m")
        p.op("dve", lambda e: e.tensor_scalar(ST[:, 4:8], ST[:, 4:8], 1.0 / D, None, AL.mult),
             reads=[f"{stk}:b"], writes=[f"{stk}:b"], tag=f"{tag}_ex2")
        p.op("dve", lambda e: e.tensor_tensor(ST[:, 12:16], ST[:, 8:12], ST[:, 8:12], AL.mult),
             reads=[f"{stk}:m"], writes=[f"{stk}:t"], tag=f"{tag}_m2")
        p.op("dve", lambda e: e.tensor_tensor(ST[:, 4:8], ST[:, 4:8], ST[:, 12:16], AL.subtract),
             reads=[f"{stk}:b", f"{stk}:t"], writes=[f"{stk}:b"], tag=f"{tag}_var")
        p.op("dve", lambda e: e.tensor_scalar(ST[:, 4:8], ST[:, 4:8], 1e-5, None, AL.add),
             reads=[f"{stk}:b"], writes=[f"{stk}:b"], tag=f"{tag}_eps")
        p.op("act", lambda e: e.activation(ST[:, 4:8], ST[:, 4:8], AF.Ln),
             reads=[f"{stk}:b"], writes=[f"{stk}:b"], tag=f"{tag}_ln")
        p.op("act", lambda e: e.activation(ST[:, 4:8], ST[:, 4:8], AF.Exp, scale=-0.5),
             reads=[f"{stk}:b"], writes=[f"{stk}:b"], tag=f"{tag}_rstd")
        p.op("dve", lambda e: e.tensor_tensor(ST[:, 12:16], ST[:, 8:12], ST[:, 4:8], AL.mult),
             reads=[f"{stk}:b", f"{stk}:m"], writes=[f"{stk}:t"], tag=f"{tag}_mr")
        p.op("dve", lambda e: e.tensor_scalar(ST[:, 12:16], ST[:, 12:16], -1.0, None, AL.mult),
             reads=[f"{stk}:t"], writes=[f"{stk}:t"], tag=f"{tag}_nmr")
        xckey = xcpfx
        for st_ in range(4):
            sl = st_ % 2
            if st_ < 2:
                p.op("act", lambda e, st_=st_, sl=sl: e.activation(
                    XCS[:, sl, :], X[:, st_, :], AF.Identity,
                    bias=ST[:, 12 + st_:13 + st_], scale=ST[:, 4 + st_:5 + st_]),
                     reads=[f"{xk}:{st_}", f"{stk}:b", f"{stk}:t"], writes=[f"{xckey}:{sl}"],
                     tag=f"{tag}_cast{st_}")
            else:
                p.op("dve", lambda e, st_=st_, sl=sl: e.tensor_scalar(
                    XCS[:, sl, :], X[:, st_, :], ST[:, 4 + st_:5 + st_],
                    ST[:, 12 + st_:13 + st_], AL.mult, AL.add),
                     reads=[f"{xk}:{st_}", f"{stk}:b", f"{stk}:t"], writes=[f"{xckey}:{sl}"],
                     tag=f"{tag}_cast{st_}")
            p.op("scd", lambda e, st_=st_, sl=sl: e.dma_start(hrow[:, st_, :], XCS[:, sl, :]),
                 reads=[f"{xckey}:{sl}"], writes=[hkey], tag=f"{tag}_hst{st_}", ch=f"{xckey}:{sl}")

    def emit_ffn(hh, hkey, grow, X, xk, XT, MLT, LTM, GT8, GT, GM, tag):
        for dc in range(DC):
            sl = dc % 8
            p.op("syd",
                 lambda e, dc=dc, sl=sl: e.dma_start_transpose(
                     XT[:, sl, :], hh[grow:grow + 512, dc * 128:(dc + 1) * 128]),
                 reads=[hkey], writes=[f"ffxt:{sl}"], tag=f"{tag}_tld{dc}")
            p.op("pe", lambda e, dc=dc, sl=sl: e.matmul(
                P4[0:ERL, :], atw_sb[:, dc * ERL:(dc + 1) * ERL], XT[:, sl, :],
                start=(dc == 0), stop=(dc == DC - 1)),
                 reads=[f"ffxt:{sl}", "atw"], writes=["P4"], tag=f"{tag}_mid{dc}")
        p.op("dve", lambda e: e.tensor_copy(MLT[:], P4[0:ERL, :]),
             reads=["P4"], writes=["ffmlt"], tag=f"{tag}_mide")
        for s in range(4):
            p.op("pe", lambda e, s=s: e.transpose(
                PQ[0:128, s * 4:s * 4 + 4], MLT[ER:ERL, s * 128:(s + 1) * 128], idf_sb[ER:ERL, ER:ERL]),
                 reads=["ffmlt", "idf"], writes=["PQ0"], tag=f"{tag}_gtr{s}")
        p.op("dve", lambda e: e.tensor_copy(LTM[:, 0:16], PQ[:, 0:16]),
             reads=["PQ0"], writes=["ffg"], tag=f"{tag}_gte")
        L = LTM[:, 0:16]
        W = LTM[:, 16:96]
        g = ["ffg"]

        def gop(emit, t, act=False):
            p.op("act" if act else "dve", emit, reads=g, writes=g, tag=f"{tag}_{t}")

        gop(lambda e: e.tensor_tensor(W[:, 0:4], L[:, 0::4], L[:, 1::4], AL.max), "mab")
        gop(lambda e: e.tensor_tensor(W[:, 4:8], L[:, 0::4], L[:, 1::4], AL.min), "nab")
        gop(lambda e: e.tensor_tensor(W[:, 8:12], L[:, 2::4], L[:, 3::4], AL.max), "mcd")
        gop(lambda e: e.tensor_tensor(W[:, 12:16], L[:, 2::4], L[:, 3::4], AL.min), "ncd")
        gop(lambda e: e.tensor_tensor(W[:, 16:20], W[:, 0:4], W[:, 8:12], AL.min), "t1")
        gop(lambda e: e.tensor_tensor(W[:, 20:24], W[:, 4:8], W[:, 12:16], AL.max), "t2")
        gop(lambda e: e.tensor_tensor(W[:, 24:28], W[:, 16:20], W[:, 20:24], AL.max), "m2")
        gop(lambda e: e.activation(W[:, 32:48], L[:], AF.Exp), "gexp", act=True)
        for ei in range(4):
            gop(lambda e, ei=ei: e.tensor_tensor(W[:, 48 + ei * 4:52 + ei * 4],
                                                 L[:, ei::4], W[:, 24:28], AL.is_ge), f"ge{ei}")
            gop(lambda e, ei=ei: e.tensor_tensor(W[:, 64 + ei * 4:68 + ei * 4],
                                                 W[:, 48 + ei * 4:52 + ei * 4],
                                                 W[:, 32 + ei:48:4], AL.mult), f"gm{ei}")
        gop(lambda e: e.tensor_tensor(W[:, 0:4], W[:, 64:68], W[:, 68:72], AL.add), "z1")
        gop(lambda e: e.tensor_tensor(W[:, 4:8], W[:, 72:76], W[:, 76:80], AL.add), "z2")
        gop(lambda e: e.tensor_tensor(W[:, 0:4], W[:, 0:4], W[:, 4:8], AL.add), "z")
        gop(lambda e: e.reciprocal(W[:, 4:8], W[:, 0:4]), "rz")
        for ei in range(4):
            gop(lambda e, ei=ei: e.tensor_tensor(W[:, 48 + ei * 4:52 + ei * 4],
                                                 W[:, 64 + ei * 4:68 + ei * 4],
                                                 W[:, 4:8], AL.mult), f"gn{ei}")
        for s in range(4):
            p.op("dve", lambda e, s=s: e.tensor_copy(GT8[:, s * 4:s * 4 + 4],
                                                     W[:, 48 + s:64:4]),
                 reads=g, writes=["ffgt8"], tag=f"{tag}_gc{s}")
            p.op("pe", lambda e, s=s: e.transpose(
                PB[0:4, s * 128:(s + 1) * 128], GT8[:, s * 4:s * 4 + 4], idb_sb[:]),
                 reads=["ffgt8", "idb"], writes=["PB"], tag=f"{tag}_gbt{s}")
        p.op("dve", lambda e: e.tensor_copy(GT[:], PB[0:4, 0:512]),
             reads=["PB"], writes=["ffgt"], tag=f"{tag}_gtev")
        p.op("pe", lambda e: e.matmul(P4[0:ER, :], e4_sb[:], GT[:], start=True, stop=True),
             reads=["ffgt", "e4"], writes=["P4"], tag=f"{tag}_g64")
        p.op("dve", lambda e: e.tensor_tensor(GM[:], MLT[0:ER, :], P4[0:ER, :], AL.mult),
             reads=["P4", "ffmlt"], writes=["ffgm"], tag=f"{tag}_gmid")
        for s in range(4):
            for oc in range(DC // 4):
                slp = (s * (DC // 4) + oc) % 2
                p.op("pe", lambda e, s=s, oc=oc, slp=slp: e.matmul(
                    PQ[:, slp * 512:(slp + 1) * 512], GM[:, s * 128:(s + 1) * 128],
                    bb_sb[:, oc * 512:(oc + 1) * 512], start=True, stop=True),
                     reads=["ffgm", "bb"], writes=[f"PQ{slp}"], tag=f"{tag}_d{s}_{oc}")
                p.op("dve", lambda e, s=s, oc=oc, slp=slp: e.tensor_tensor(
                    X[:, s, oc * 512:(oc + 1) * 512], X[:, s, oc * 512:(oc + 1) * 512],
                    PQ[:, slp * 512:(slp + 1) * 512], AL.add),
                     reads=[f"PQ{slp}", f"{xk}:{s}"], writes=[f"{xk}:{s}"],
                     tag=f"{tag}_r{s}_{oc}")

    ph1 = sb.mark()
    Xg = [sb.alloc(f"X{i}", [128, 4, D], F32) for i in range(2)]
    XCS1 = [sb.alloc(f"XCS1_{i}", [128, 2, D], BF16) for i in range(2)]
    TRASH = sb.alloc("TRASH", [128, D], BF16)
    STg = [sb.alloc(f"STg{i}", [128, 16], F32) for i in range(2)]
    XTr = sb.alloc("XTr", [128, 8, 512], BF16)
    MLT1 = sb.alloc("MLT1", [ERL, 512], F32)
    LTM1 = sb.alloc("LTM1", [128, 96], F32)
    GT8a = sb.alloc("GT8a", [128, 16], BF16)
    GT1 = sb.alloc("GT1", [4, 512], BF16)
    GM1 = sb.alloc("GM1", [ER, 512], BF16)

    x0r = x0.rearrange("(gg s p) d -> gg p s d", s=4, p=128)
    h1r = h1h.rearrange("(gg s p) d -> gg p s d", s=4, p=128)
    h2r = h2h.rearrange("(gg s p) d -> gg p s d", s=4, p=128)
    x1r = x1h.rearrange("(gg s p) d -> gg p s d", s=4, p=128)

    def load_group(g):
        bi = g % 2
        for st in range(4):
            p.op("syd", lambda e, g=g, bi=bi, st=st: e.dma_start(
                Xg[bi][:, st, :], x0r[g, :, st, :]),
                 writes=[f"x{bi}:{st}"], tag=f"1a_ldx{g}_{st}", ch=f"x{bi}:{st}")

    def stage_a(g):
        bi = g % 2
        emit_ln_cast(Xg[bi], XCS1[bi], TRASH, STg[bi], f"x{bi}", f"st{bi}",
                     h1r[g], f"h1h:{g}", f"1a_ln1g{g}", f"xcs{bi}")

    def stage_b(g):
        bi = g % 2
        X = Xg[bi]
        xk = f"x{bi}"
        emit_ffn(h1h, f"h1h:{g}", g * 512, X, xk, XTr, MLT1, LTM1, GT8a, GT1, GM1, f"f1g{g}")
        p.op("scd", lambda e, g=g, X=X: e.dma_start(x1r[g], X[:]),
             reads=[f"{xk}:{st}" for st in range(4)], writes=["x1h"],
             tag=f"1a_stx1g{g}", ch=f"{xk}:st")
        emit_ln_cast(X, XCS1[bi], TRASH, STg[bi], xk, f"st{bi}",
                     h2r[g], "h2h:0", f"1a_ln2g{g}", f"xcs{bi}")

    load_group(0)
    if GO > 1:
        load_group(1)
    stage_a(0)
    for g in range(GO):
        if g + 1 < GO:
            stage_a(g + 1)
        stage_b(g)
        if g + 2 < GO:
            load_group(g + 2)
        p.barrier()
    sb.reset(ph1)

    ph1b = sb.mark()
    H2T = sb.alloc("H2T", [128, DC, SGW], BF16)
    Wb = [sb.alloc(f"Wb{i}", [128, DC * 128], BF16) for i in range(2)]
    QS = [sb.alloc(f"QS{i}", [128, SGW], BF16) for i in range(2)]

    SGW_O = OWN
    for dc in range(DC):
        p.op("syd", lambda e, dc=dc: e.dma_start_transpose(
            H2T[:, dc, 0:SGW_O], h2h[0:SGW_O, dc * 128:(dc + 1) * 128]),
             reads=["h2h:0"], writes=[f"h2t:{dc}"], tag=f"1b_tld{dc}")
    tgts = [(wk, oc, kin[oc * 128:(oc + 1) * 128, :], "kvk") for oc in range(KVH)]
    tgts += [(wv, oc, vin[oc * 128:(oc + 1) * 128, :], "kvv") for oc in range(KVH)]
    tgts += [(wq, oc, qth[oc], "qh") for oc in range(HEADS)]
    RG = [[0, 1], [2, 3], [4, 5], [6, 7]]
    for ti, (wsrc, oc, dsth, wkind) in enumerate(tgts):
        wbi = ti % 2
        ntc = OWN // 512
        p.op("gpd", lambda e, wsrc=wsrc, oc=oc, wbi=wbi: e.dma_start(
            Wb[wbi].rearrange("p (dc q) -> p dc q", q=128),
            wsrc.rearrange("(dc p) q -> p dc q", p=128)[:, :, oc * 128:(oc + 1) * 128]),
             writes=[f"wb{wbi}"], tag=f"1b_lw{ti}", ch=f"wbl{wbi}")
        for tc in range(ntc):
            pso = tc % 2
            psk = "P4" if pso == 0 else "P5"
            PS = P4 if pso == 0 else P5
            for dc in range(DC):
                p.op("pe", lambda e, wbi=wbi, dc=dc, tc=tc, PS=PS: e.matmul(
                    PS[:], Wb[wbi][:, dc * 128:(dc + 1) * 128],
                    H2T[:, dc, tc * 512:(tc + 1) * 512],
                    start=(dc == 0), stop=(dc == DC - 1)),
                     reads=[f"wb{wbi}", f"h2t:{dc}"], writes=[psk],
                     tag=f"1b_mm{ti}_{tc}_{dc}")
            p.op("act", lambda e, tc=tc, wbi=wbi, PS=PS: e.copy(
                QS[wbi][:, tc * 512:(tc + 1) * 512], PS[:]),
                 reads=[psk], writes=[f"qs{wbi}:{tc}"], tag=f"1b_ev{ti}_{tc}")
        p.op("syd", lambda e, dsth=dsth, wbi=wbi, ntc=ntc: e.dma_start(
            dsth[:, 0:ntc * 512], QS[wbi][:, 0:ntc * 512]),
             reads=[f"qs{wbi}:{tc}" for tc in range(ntc)],
             writes=[{"kvk": "kin", "kvv": "vin"}.get(wkind, f"{wkind}:{oc}")],
             tag=f"1b_st{ti}", ch=f"qs{wbi}:0")
        if wkind == "kvk" and oc == KVH - 1:
            p.op("pool", lambda e: e.collective_compute(
                "AllGather", mybir.AluOpType.bypass, replica_groups=RG,
                ins=[kin.opt()], outs=[kout.opt()]),
                 reads=["kin"], writes=["kout"], tag="1b_cck")
        if wkind == "kvv" and oc == KVH - 1:
            p.op("pool", lambda e: e.collective_compute(
                "AllGather", mybir.AluOpType.bypass, replica_groups=RG,
                ins=[vin.opt()], outs=[vout.opt()]),
                 reads=["vin"], writes=["vout"], tag="1b_ccv")
    p.barrier()
    sb.reset(ph1b)

    ph2 = sb.mark()
    KTs = [sb.alloc(f"KT{i}", [128, SEQ], BF16) for i in range(2)]
    VTs = [sb.alloc(f"VT{i}", [128, KC, 128], BF16) for i in range(2)]
    QTs = [sb.alloc(f"QT{i}", [128, OWN], BF16) for i in range(2)]
    PTq = [sb.alloc(f"PT{i}", [128, 1024], BF16) for i in range(4)]
    RZf = [sb.alloc(f"RZf{i}", [128, 512], F32) for i in range(2)]
    OS = [sb.alloc(f"OS{i}", [128, 512], BF16) for i in range(2)]

    KPR = KC // 2

    def kv_src(kind, kv, r):
        return r * KVH * 128 + kv * 128

    def load_kv(kv):
        bi = kv % 2
        for r in range(2):
            p.op("syd", lambda e, kv=kv, bi=bi, r=r: e.dma_start(
                KTs[bi][:, r * OWN:(r + 1) * OWN],
                kout[kv_src("k", kv, r):kv_src("k", kv, r) + 128, :]),
                 reads=["kout"], writes=[f"kt{bi}"], tag=f"2_ldk{kv}_{r}")
        for kc in range(KC):
            r, kcl = kc // KPR, kc % KPR
            p.op("syd", lambda e, kv=kv, kc=kc, bi=bi, r=r, kcl=kcl: e.dma_start_transpose(
                VTs[bi][:, kc, :],
                vout[kv_src("v", kv, r) + 0:kv_src("v", kv, r) + 128,
                     kcl * 128:(kcl + 1) * 128]),
                 reads=["vout"], writes=[f"vt{bi}"], tag=f"2_ldv{kv}_{kc}")

    def load_q(h):
        bi = h % 2
        p.op("syd", lambda e, h=h, bi=bi: e.dma_start(QTs[bi][:], qth[h]),
             reads=[f"qh:{h}"], writes=[f"qt{bi}"], tag=f"2_ldq{h}")

    load_kv(0)
    load_q(0)
    for kv in range(KVH):
        kbi = kv % 2
        if kv + 1 < KVH:
            load_kv(kv + 1)
        KT, VT = KTs[kbi], VTs[kbi]
        for qh in range(QPK):
            h = kv * QPK + qh
            qbi = h % 2
            if h + 1 < HEADS:
                load_q(h + 1)
            QT = QTs[qbi]
            for qc in range(QC):
                okey = "P4" if qc % 2 == 0 else "P6"
                OP = P4 if qc % 2 == 0 else P6
                NPAIR = KC // 2

                def s_pair(pr):
                    rq = pr % 2
                    r = pr % 4
                    for i in range(2):
                        kc = pr * 2 + i
                        p.op("pe", lambda e, kc=kc, i=i, rq=rq, qc=qc, KT=KT, QT=QT: e.matmul(
                            PQ[:, rq * 1024 + i * 512:rq * 1024 + (i + 1) * 512],
                            KT[:, kc * 128:(kc + 1) * 128],
                            QT[:, qc * 512:(qc + 1) * 512], start=True, stop=True),
                             reads=[f"kt{kbi}", f"qt{qbi}"], writes=[f"PQr{rq}"],
                             tag=f"2_s{h}_{qc}_{kc}")
                    p.op("act", lambda e, r=r, rq=rq: e.activation(
                        PTq[r][:, 0:1024], PQ[:, rq * 1024:(rq + 1) * 1024],
                        AF.Exp, scale=SCL),
                         reads=[f"PQr{rq}"], writes=[f"pt{r}"], tag=f"2_e{h}_{qc}_{pr}")

                def pvz_pair(pr):
                    r = pr % 4
                    for i in range(2):
                        kc = pr * 2 + i
                        p.op("pe", lambda e, kc=kc, i=i, r=r, VT=VT, OP=OP: e.matmul(
                            OP[:], VT[:, kc, :], PTq[r][:, i * 512:(i + 1) * 512],
                            start=(kc == 0), stop=(kc == KC - 1)),
                             reads=[f"vt{kbi}", f"pt{r}"], writes=[okey],
                             tag=f"2_pv{h}_{qc}_{kc}")
                    for i in range(2):
                        kc = pr * 2 + i
                        p.op("pe", lambda e, kc=kc, i=i, r=r: e.matmul(
                            P5[:], ones_sb[:], PTq[r][:, i * 512:(i + 1) * 512],
                            start=(kc == 0), stop=(kc == KC - 1)),
                             reads=["ones", f"pt{r}"], writes=["P5"],
                             tag=f"2_z{h}_{qc}_{kc}")

                s_pair(0)
                s_pair(1)
                for pr in range(2, NPAIR):
                    s_pair(pr)
                    pvz_pair(pr - 2)
                pvz_pair(NPAIR - 2)
                pvz_pair(NPAIR - 1)
                ob = (h * QC + qc) % 2
                p.op("dve", lambda e, ob=ob: e.reciprocal(RZf[ob][:], P5[:]),
                     reads=["P5"], writes=[f"rzf{ob}"], tag=f"2_rz{h}_{qc}")
                p.op("dve", lambda e, ob=ob, OP=OP: e.tensor_tensor(
                    OS[ob][:], OP[:], RZf[ob][:], AL.mult),
                     reads=[okey, f"rzf{ob}"], writes=[f"os{ob}"], tag=f"2_oe{h}_{qc}")
                p.op("syd", lambda e, h=h, qc=qc, ob=ob: e.dma_start(oh[qc, h], OS[ob][:]),
                     reads=[f"os{ob}"], writes=[f"oh:{qc}"], tag=f"2_ost{h}_{qc}", ch=f"os{ob}")
    p.barrier()
    sb.reset(ph2)

    ph3 = sb.mark()
    OQ = [sb.alloc(f"OQ{i}", [128, HEADS, 512], BF16) for i in range(2)]
    WOb = [sb.alloc(f"WOb{i}", [128, DC * 512], BF16) for i in range(2)]
    X1t = [sb.alloc(f"X1t{i}", [128, 512], F32) for i in range(2)]
    X2s = [sb.alloc(f"X2s{i}", [128, 512], F32) for i in range(2)]

    def load_wo(jc):
        wbi = jc % 2
        p.op("gpd", lambda e, jc=jc, wbi=wbi: e.dma_start(
            WOb[wbi].rearrange("p (dc q) -> p dc q", q=512),
            wo.rearrange("(dc p) q -> p dc q", p=128)[:, :, jc * 512:(jc + 1) * 512]),
             writes=[f"wob{wbi}"], tag=f"3_lw{jc}", ch=f"wobl{wbi}")

    def load_oq(jc, qc):
        bi = qc % 2
        p.op("syd", lambda e, qc=qc, bi=bi: e.dma_start(
            OQ[bi][:], oh[qc].rearrange("hh p q -> p hh q")),
             reads=[f"oh:{qc}"], writes=[f"oq{bi}"], tag=f"3_ldo{jc}_{qc}")

    load_wo(0)
    for jc in range(JCN):
        wbi = jc % 2
        if jc + 1 < JCN:
            load_wo(jc + 1)
        load_oq(jc, 0)
        for qci in range(QC):
            obi = qci % 2
            if qci + 1 < QC:
                load_oq(jc, qci + 1)
            for s in range(4):
                qt = qci * 4 + s
                pso = qt % 2
                psk = "P4" if pso == 0 else "P5"
                PS = P4 if pso == 0 else P5
                for hd in range(HEADS):
                    p.op("pe", lambda e, hd=hd, s=s, obi=obi, wbi=wbi, PS=PS: e.matmul(
                        PS[:], OQ[obi][:, hd, s * 128:(s + 1) * 128],
                        WOb[wbi][:, hd * 512:(hd + 1) * 512],
                        start=(hd == 0), stop=(hd == HEADS - 1)),
                         reads=[f"oq{obi}", f"wob{wbi}"], writes=[psk],
                         tag=f"3_mm{jc}_{qt}_{hd}")
                xb = qt % 2
                p.op("scd", lambda e, qt=qt, jc=jc, xb=xb: e.dma_start(
                    X1t[xb][:], x1h[qt * 128:(qt + 1) * 128, jc * 512:(jc + 1) * 512]),
                     reads=["x1h"], writes=[f"x1t{xb}"], tag=f"3_ldx1_{jc}_{qt}")
                p.op("dve", lambda e, xb=xb, PS=PS: e.tensor_tensor(
                    X2s[xb][:], X1t[xb][:], PS[:], AL.add),
                     reads=[f"x1t{xb}", psk], writes=[f"x2s{xb}"], tag=f"3_add{jc}_{qt}")
                p.op("syd", lambda e, qt=qt, jc=jc, xb=xb: e.dma_start(
                    x2h[qt * 128:(qt + 1) * 128, jc * 512:(jc + 1) * 512], X2s[xb][:]),
                     reads=[f"x2s{xb}"], writes=[f"x2h:{qt // 4}"], tag=f"3_stx2_{jc}_{qt}", ch=f"x2s{xb}")
    p.barrier()
    sb.reset(ph3)

    ph4 = sb.mark()
    Xg4 = [sb.alloc(f"X4_{i}", [128, 4, D], F32) for i in range(2)]
    XCS4 = [sb.alloc(f"XCS4_{i}", [128, 2, D], BF16) for i in range(2)]
    TRASH4 = sb.alloc("TRASH4", [128, D], BF16)
    STg4 = [sb.alloc(f"ST4{i}", [128, 16], F32) for i in range(2)]
    XTr4 = sb.alloc("XTr4", [128, 8, 512], BF16)
    MLT4 = sb.alloc("MLT4", [ERL, 512], F32)
    LTM4 = sb.alloc("LTM4", [128, 96], F32)
    GT8b = sb.alloc("GT8b", [128, 16], BF16)
    GT4 = sb.alloc("GT4", [4, 512], BF16)
    GM4 = sb.alloc("GM4", [ER, 512], BF16)
    x2r = x2h.rearrange("(gg s p) d -> gg p s d", s=4, p=128)
    h3r = h3h.rearrange("(gg s p) d -> gg p s d", s=4, p=128)
    outr = out.rearrange("(gg s p) d -> gg p s d", s=4, p=128)

    def load_group4(g):
        bi = g % 2
        for st in range(4):
            p.op("syd", lambda e, g=g, bi=bi, st=st: e.dma_start(
                Xg4[bi][:, st, :], x2r[g, :, st, :]),
                 reads=[f"x2h:{g}"], writes=[f"x4{bi}:{st}"], tag=f"4_ldx{g}_{st}",
                 ch=f"x4{bi}:{st}")

    def stage_a4(g):
        bi = g % 2
        emit_ln_cast(Xg4[bi], XCS4[bi], TRASH4, STg4[bi], f"x4{bi}", f"st4{bi}",
                     h3r[g], f"h3h:{g}", f"4_ln3g{g}", f"xcs4{bi}")

    def stage_b4(g):
        bi = g % 2
        X = Xg4[bi]
        xk = f"x4{bi}"
        emit_ffn(h3h, f"h3h:{g}", g * 512, X, xk, XTr4, MLT4, LTM4, GT8b, GT4, GM4, f"f2g{g}")
        p.op("syd", lambda e, g=g, X=X: e.dma_start(outr[g], X[:]),
             reads=[f"{xk}:{st}" for st in range(4)], tag=f"4_stout{g}", ch=f"{xk}:0")

    load_group4(0)
    if GO > 1:
        load_group4(1)
    stage_a4(0)
    for g in range(GO):
        if g + 1 < GO:
            stage_a4(g + 1)
        stage_b4(g)
        if g + 2 < GO:
            load_group4(g + 2)
        p.barrier()

    with nc.Block() as blk:
        p.materialize(blk)
    return nc, p


def _prep_host(x, wqn, wkn, wvn, won, A, B, Wr, cfg):
    At = np.ascontiguousarray(A.transpose(2, 0, 1).reshape(cfg["D"], 64))
    WrT = np.ascontiguousarray(Wr.T)
    AWr = np.ascontiguousarray(np.concatenate([At, WrT], axis=1))
    Bf = np.ascontiguousarray(B.transpose(0, 2, 1).reshape(64, cfg["D"]))
    return AWr, Bf


def kernel(L0_attn_x, L0_attn_wq, L0_attn_wk, L0_attn_wv, L0_attn_wo,
           L0_ffn_A, L0_ffn_B, L0_ffn_Wr):
    x = np.ascontiguousarray(np.asarray(L0_attn_x, dtype=np.float32))
    wqn = np.ascontiguousarray(np.asarray(L0_attn_wq, dtype=np.float32))
    wkn = np.ascontiguousarray(np.asarray(L0_attn_wk, dtype=np.float32))
    wvn = np.ascontiguousarray(np.asarray(L0_attn_wv, dtype=np.float32))
    won = np.ascontiguousarray(np.asarray(L0_attn_wo, dtype=np.float32))
    A = np.asarray(L0_ffn_A, dtype=np.float32)
    B = np.asarray(L0_ffn_B, dtype=np.float32)
    Wr = np.asarray(L0_ffn_Wr, dtype=np.float32)
    AWr, Bf = _prep_host(x, wqn, wkn, wvn, won, A, B, Wr, CFG_FULL)

    nc, _ = build_program(CFG_FULL)
    in_maps = []
    for c in range(8):
        b, h = c // 2, c % 2
        xb = x[b, h * 2048:(h + 1) * 2048]
        in_maps.append({"x0": np.ascontiguousarray(xb), "wq": wqn, "wk": wkn,
                        "wv": wvn, "wo": won, "awr": AWr, "bfl": Bf})
    res = run_bass_kernel_spmd(nc, in_maps, core_ids=list(range(8)))
    outp = np.empty((4, 4096, 4096), dtype=np.float32)
    for c in range(8):
        b, h = c // 2, c % 2
        outp[b, h * 2048:(h + 1) * 2048] = res.results[c]["out"]
    return outp


# version 21 (final)
# speedup vs baseline: 1.1464x; 1.1464x over previous
import math
import numpy as np
import ml_dtypes
import concourse.bass as bass
import concourse.mybir as mybir
from concourse.bass_utils import run_bass_kernel_spmd

F32 = mybir.dt.float32
BF16 = mybir.dt.bfloat16
AF = mybir.ActivationFunctionType
AL = mybir.AluOpType
AX = mybir.AxisListType.X
bf16 = ml_dtypes.bfloat16

QUEUES = ("pe", "act", "dve", "pool", "syd", "gpd", "scd")
ENGINE_OF = {"pe": "tensor", "act": "scalar", "dve": "vector", "pool": "gpsimd",
             "syd": "sync", "gpd": "gpsimd", "scd": "scalar"}
DMA_QUEUES = {"syd", "gpd", "scd"}


class Op:
    __slots__ = ("queue", "ch", "emit", "reads", "writes", "deps", "need_inc",
                 "inc_idx", "waits", "tag")

    def __init__(self, queue, ch, emit, reads, writes, tag):
        self.queue = queue
        self.ch = ch
        self.emit = emit
        self.reads = reads
        self.writes = writes
        self.deps = set()
        self.need_inc = False
        self.inc_idx = None
        self.waits = ()
        self.tag = tag


class Prog:

    def __init__(self, nc):
        self.nc = nc
        self.ops = []
        self.key_w = {}
        self.key_r = {}
        self.last_op = {}
        self.nsem = 0

    def op(self, queue, emit, reads=(), writes=(), tag="", ch=None):
        if ch is None:
            if queue in DMA_QUEUES:
                ch = writes[0] if writes else reads[0]
            else:
                ch = queue
        o = Op(queue, ch, emit, tuple(reads), tuple(writes), tag)
        for k in o.reads:
            for w in self.key_w.get(k, ()):
                o.deps.add(w)
        for k in o.writes:
            for w in self.key_w.get(k, ()):
                o.deps.add(w)
            for r in self.key_r.get(k, ()):
                o.deps.add(r)
        for k in o.writes:
            self.key_w[k] = [o]
            self.key_r[k] = []
        for k in o.reads:
            self.key_r.setdefault(k, []).append(o)
        o.deps.discard(o)
        self.ops.append(o)
        self.last_op[o.ch] = o
        return o

    def barrier(self):
        lasts = [op for op in self.last_op.values() if op is not None]
        for q in QUEUES:
            o = Op(q, q, (lambda e: e.nop()), (), (), "barrier")
            o.deps.update(lasts)
            self.ops.append(o)
            self.last_op[q] = o
        self.key_w.clear()
        self.key_r.clear()

    @staticmethod
    def _skip(d, o):
        return d.queue == "pe" and o.queue == "pe"

    def materialize(self, block):
        for o in self.ops:
            if o.queue in DMA_QUEUES or o.queue == "pool":
                o.need_inc = True
            for d in o.deps:
                if not self._skip(d, o):
                    d.need_inc = True
        cnt = {}
        for o in self.ops:
            if o.need_inc:
                cnt[o.ch] = cnt.get(o.ch, 0) + 1
                o.inc_idx = cnt[o.ch]
        sems = {c: self.nc.alloc_semaphore(f"s{i}")
                for i, c in enumerate(sorted(cnt))}
        assert len(sems) <= 100, f"too many sem channels: {len(sems)}"
        dma_ch = {o.ch for o in self.ops if o.queue in DMA_QUEUES}
        clock = {}
        snap = {}
        for o in self.ops:
            oc = clock.setdefault(o.ch, {})
            needed = {}
            for d in o.deps:
                if self._skip(d, o):
                    continue
                assert d.inc_idx is not None, (d.tag, o.tag)
                if oc.get(d.ch, 0) < d.inc_idx:
                    needed[d.ch] = max(needed.get(d.ch, 0), d.inc_idx)
            o.waits = tuple(sorted(needed.items()))
            for pq, v in o.waits:
                for k2, v2 in snap[(pq, v)].items():
                    if oc.get(k2, 0) < v2:
                        oc[k2] = v2
                if oc.get(pq, 0) < v:
                    oc[pq] = v
            if o.need_inc:
                sn = dict(oc)
                sn[o.ch] = o.inc_idx
                snap[(o.ch, o.inc_idx)] = sn
        by_engine = {}
        for o in self.ops:
            by_engine.setdefault(ENGINE_OF[o.queue], []).append(o)
        for engine, ops in by_engine.items():
            def body(e, ops=ops):
                for o in ops:
                    for pq, v in o.waits:
                        e.wait_ge(sems[pq], v * (16 if pq in dma_ch else 1))
                    inst = o.emit(e)
                    if o.need_inc:
                        inst.then_inc(sems[o.ch],
                                      16 if o.ch in dma_ch else 1)
            getattr(block, engine)(body)


class SbufAlloc:
    def __init__(self, nc):
        self.nc = nc
        self.off = 16640
        self.n = 0

    def alloc(self, name, shape, dtype):
        t = self.nc.alloc_sbuf_tensor_at(f"{name}_{self.n}", list(shape), dtype,
                                         offset=self.off)
        self.n += 1
        per_part = int(np.prod(shape[1:])) * mybir.dt.size(dtype)
        self.off += (per_part + 31) & ~31
        assert self.off <= 229376 - 64, f"SBUF overflow at {name}: {self.off}"
        return t

    def mark(self):
        return self.off

    def reset(self, m):
        self.off = m


CFG_FULL = dict(D=4096, SEQ=4096, OWN=2048, HEADS=32, KVH=8)
CFG_SMALL = dict(D=512, SEQ=1024, OWN=512, HEADS=4, KVH=1)


def build_program(cfg):
    D = cfg["D"]; SEQ = cfg["SEQ"]; OWN = cfg["OWN"]
    HEADS = cfg["HEADS"]; KVH = cfg["KVH"]
    HD = 128
    NEXP, LORA = 4, 16
    DC = D // 128
    ER = NEXP * LORA
    ERL = ER + NEXP
    GF = SEQ // 512
    GO = OWN // 512
    QPK = HEADS // KVH
    KC = SEQ // 128
    NQUAD = (KC + 3) // 4
    DKV = KVH * HD
    JCN = D // 512
    SCL = 1.0 / math.sqrt(HD)
    NSG = max(1, SEQ // 2048)
    SGW = SEQ // NSG
    TCN = SGW // 512
    QC = OWN // 512

    nc = bass.Bass("TRN2", target_bir_lowering=False)

    x0 = nc.dram_tensor("x0", [OWN, D], F32, kind="ExternalInput").ap()
    wq = nc.dram_tensor("wq", [D, D], F32, kind="ExternalInput").ap()
    wk = nc.dram_tensor("wk", [D, DKV], F32, kind="ExternalInput").ap()
    wv = nc.dram_tensor("wv", [D, DKV], F32, kind="ExternalInput").ap()
    wo = nc.dram_tensor("wo", [D, D], F32, kind="ExternalInput").ap()
    awr = nc.dram_tensor("awr", [D, ERL], F32, kind="ExternalInput").ap()
    bfl = nc.dram_tensor("bfl", [ER, D], F32, kind="ExternalInput").ap()
    out = nc.dram_tensor("out", [OWN, D], F32, kind="ExternalOutput").ap()

    h1h = nc.dram_tensor("h1h", [OWN, D], BF16, kind="Internal").ap()
    h2h = nc.dram_tensor("h2h", [OWN, D], BF16, kind="Internal").ap()
    h3h = nc.dram_tensor("h3h", [OWN, D], BF16, kind="Internal").ap()
    x1h = nc.dram_tensor("x1h", [OWN, D], F32, kind="Internal").ap()
    x2h = nc.dram_tensor("x2h", [OWN, D], F32, kind="Internal").ap()
    qth = nc.dram_tensor("qth", [HEADS, 128, OWN], BF16, kind="Internal").ap()
    kin = nc.dram_tensor("kin", [KVH * 128, OWN], BF16, kind="Internal").ap()
    vin = nc.dram_tensor("vin", [KVH * 128, OWN], BF16, kind="Internal").ap()
    kout = nc.dram_tensor("kout", [2 * KVH * 128, OWN], BF16, kind="Internal").ap()
    vout = nc.dram_tensor("vout", [2 * KVH * 128, OWN], BF16, kind="Internal").ap()
    oh = nc.dram_tensor("oh", [QC, HEADS, 128, 512], BF16, kind="Internal").ap()

    onesc = nc.inline_tensor(np.ones((128, 128), dtype=bf16), name="c_ones").ap()
    idbc = nc.inline_tensor(np.eye(128, dtype=bf16), name="c_idb").ap()
    idfc = nc.inline_tensor(np.eye(128, dtype=np.float32), name="c_idf").ap()
    E4np = np.zeros((NEXP, ER), dtype=bf16)
    for e_ in range(NEXP):
        E4np[e_, e_ * LORA:(e_ + 1) * LORA] = 1
    e4c = nc.inline_tensor(E4np, name="c_e4").ap()

    PQ = nc.alloc_psum_tensor("PQ", [128, 2048], F32)
    P4 = nc.alloc_psum_tensor("P4", [128, 512], F32)
    P5 = nc.alloc_psum_tensor("P5", [128, 512], F32)
    P6 = nc.alloc_psum_tensor("P6", [128, 512], F32)
    PB = nc.alloc_psum_tensor("PB", [128, 1024], BF16)

    p = Prog(nc)
    sb = SbufAlloc(nc)

    CZ = sb.alloc("cz", [128, 4], F32)
    CZB = sb.alloc("czb", [128, 4], BF16)
    p.op("dve", lambda e: e.memset(CZ[:, 0:1], 0.0), writes=["cz"], tag="cz0")
    p.op("dve", lambda e: e.memset(CZ[:, 1:2], 1.0), writes=["cz"], tag="cz1")
    p.op("dve", lambda e: e.memset(CZB[:, 0:1], 0.0), writes=["cz"], tag="czb0")
    p.op("dve", lambda e: e.memset(CZB[:, 1:2], 1.0), writes=["cz"], tag="czb1")
    nc.const_aps.register(F32, 0.0, CZ[:, 0:1])
    nc.const_aps.register(F32, 1.0, CZ[:, 1:2])
    nc.const_aps.register(BF16, 0.0, CZB[:, 0:1])
    nc.const_aps.register(BF16, 1.0, CZB[:, 1:2])

    ones_sb = sb.alloc("ones", [128, 128], BF16)
    idb_sb = sb.alloc("idb", [128, 128], BF16)
    idf_sb = sb.alloc("idf", [128, 128], F32)
    e4_sb = sb.alloc("e4", [NEXP, ER], BF16)
    atw_sb = sb.alloc("atw", [128, DC * ERL], BF16)
    bb_sb = sb.alloc("bb", [ER, D], BF16)

    ph0 = sb.mark()
    atwf = sb.alloc("atwf", [128, DC * ERL], F32)
    bf_f = sb.alloc("bf_f", [ER, D], F32)
    p.op("syd", lambda e: e.dma_start(ones_sb[:], onesc), writes=["ones"], tag="ld_ones")
    p.op("syd", lambda e: e.dma_start(idb_sb[:], idbc), writes=["idb"], tag="ld_idb")
    p.op("syd", lambda e: e.dma_start(idf_sb[:], idfc), writes=["idf"], tag="ld_idf")
    p.op("syd", lambda e: e.dma_start(e4_sb[:], e4c), writes=["e4"], tag="ld_e4")
    p.op("syd", lambda e: e.dma_start(
        atwf.rearrange("p (dc q) -> p dc q", q=ERL),
        awr.rearrange("(dc p) q -> p dc q", p=128)), writes=["atwf"], tag="ld_awr")
    p.op("syd", lambda e: e.dma_start(bf_f[:], bfl), writes=["bf_f"], tag="ld_bf")
    p.op("dve", lambda e: e.tensor_copy(atw_sb[:], atwf[:]),
         reads=["atwf"], writes=["atw"], tag="cast_atw")
    p.op("act", lambda e: e.mul(bb_sb[:], bf_f[:], 2.0),
         reads=["bf_f"], writes=["bb"], tag="cast_bb")
    p.barrier()
    sb.reset(ph0)

    def emit_ln_cast(X, XCS, TRASH, ST, xk, stk, hrow, hkey, tag, xcpfx):
        for st_ in range(4):
            p.op("dve", lambda e, st_=st_: e.reduce_sum(ST[:, st_:st_ + 1], X[:, st_, :], axis=AX),
                 reads=[f"{xk}:{st_}"], writes=[f"{stk}:a"], tag=f"{tag}_sum{st_}")
            p.op("act", lambda e, st_=st_: e.activation(TRASH[:], X[:, st_, :], AF.Square,
                                                        accum_out=ST[:, 4 + st_:5 + st_]),
                 reads=[f"{xk}:{st_}"], writes=["lntrash", f"{stk}:b"], tag=f"{tag}_sq{st_}")
        p.op("dve", lambda e: e.tensor_scalar(ST[:, 8:12], ST[:, 0:4], 1.0 / D, None, AL.mult),
             reads=[f"{stk}:a"], writes=[f"{stk}:m"], tag=f"{tag}_m")
        p.op("dve", lambda e: e.tensor_scalar(ST[:, 4:8], ST[:, 4:8], 1.0 / D, None, AL.mult),
             reads=[f"{stk}:b"], writes=[f"{stk}:b"], tag=f"{tag}_ex2")
        p.op("dve", lambda e: e.tensor_tensor(ST[:, 12:16], ST[:, 8:12], ST[:, 8:12], AL.mult),
             reads=[f"{stk}:m"], writes=[f"{stk}:t"], tag=f"{tag}_m2")
        p.op("dve", lambda e: e.tensor_tensor(ST[:, 4:8], ST[:, 4:8], ST[:, 12:16], AL.subtract),
             reads=[f"{stk}:b", f"{stk}:t"], writes=[f"{stk}:b"], tag=f"{tag}_var")
        p.op("dve", lambda e: e.tensor_scalar(ST[:, 4:8], ST[:, 4:8], 1e-5, None, AL.add),
             reads=[f"{stk}:b"], writes=[f"{stk}:b"], tag=f"{tag}_eps")
        p.op("act", lambda e: e.activation(ST[:, 4:8], ST[:, 4:8], AF.Ln),
             reads=[f"{stk}:b"], writes=[f"{stk}:b"], tag=f"{tag}_ln")
        p.op("act", lambda e: e.activation(ST[:, 4:8], ST[:, 4:8], AF.Exp, scale=-0.5),
             reads=[f"{stk}:b"], writes=[f"{stk}:b"], tag=f"{tag}_rstd")
        p.op("dve", lambda e: e.tensor_tensor(ST[:, 12:16], ST[:, 8:12], ST[:, 4:8], AL.mult),
             reads=[f"{stk}:b", f"{stk}:m"], writes=[f"{stk}:t"], tag=f"{tag}_mr")
        p.op("dve", lambda e: e.tensor_scalar(ST[:, 12:16], ST[:, 12:16], -1.0, None, AL.mult),
             reads=[f"{stk}:t"], writes=[f"{stk}:t"], tag=f"{tag}_nmr")
        xckey = xcpfx
        for st_ in range(4):
            sl = st_ % 2
            if st_ < 2:
                p.op("act", lambda e, st_=st_, sl=sl: e.activation(
                    XCS[:, sl, :], X[:, st_, :], AF.Identity,
                    bias=ST[:, 12 + st_:13 + st_], scale=ST[:, 4 + st_:5 + st_]),
                     reads=[f"{xk}:{st_}", f"{stk}:b", f"{stk}:t"], writes=[f"{xckey}:{sl}"],
                     tag=f"{tag}_cast{st_}")
            else:
                p.op("dve", lambda e, st_=st_, sl=sl: e.tensor_scalar(
                    XCS[:, sl, :], X[:, st_, :], ST[:, 4 + st_:5 + st_],
                    ST[:, 12 + st_:13 + st_], AL.mult, AL.add),
                     reads=[f"{xk}:{st_}", f"{stk}:b", f"{stk}:t"], writes=[f"{xckey}:{sl}"],
                     tag=f"{tag}_cast{st_}")
            p.op("scd", lambda e, st_=st_, sl=sl: e.dma_start(hrow[:, st_, :], XCS[:, sl, :]),
                 reads=[f"{xckey}:{sl}"], writes=[hkey], tag=f"{tag}_hst{st_}", ch=f"{xckey}:{sl}")

    def emit_ffn(hh, hkey, grow, X, xk, XT, MLT, LTM, GT8, GT, GM, tag):
        for dc in range(DC):
            sl = dc % 8
            p.op("syd",
                 lambda e, dc=dc, sl=sl: e.dma_start_transpose(
                     XT[:, sl, :], hh[grow:grow + 512, dc * 128:(dc + 1) * 128]),
                 reads=[hkey], writes=[f"ffxt:{sl}"], tag=f"{tag}_tld{dc}")
            p.op("pe", lambda e, dc=dc, sl=sl: e.matmul(
                P4[0:ERL, :], atw_sb[:, dc * ERL:(dc + 1) * ERL], XT[:, sl, :],
                start=(dc == 0), stop=(dc == DC - 1)),
                 reads=[f"ffxt:{sl}", "atw"], writes=["P4"], tag=f"{tag}_mid{dc}")
        p.op("dve", lambda e: e.tensor_copy(MLT[:], P4[0:ERL, :]),
             reads=["P4"], writes=["ffmlt"], tag=f"{tag}_mide")
        for s in range(4):
            p.op("pe", lambda e, s=s: e.transpose(
                PQ[0:128, s * 4:s * 4 + 4], MLT[ER:ERL, s * 128:(s + 1) * 128], idf_sb[ER:ERL, ER:ERL]),
                 reads=["ffmlt", "idf"], writes=["PQ0"], tag=f"{tag}_gtr{s}")
        p.op("dve", lambda e: e.tensor_copy(LTM[:, 0:16], PQ[:, 0:16]),
             reads=["PQ0"], writes=["ffg"], tag=f"{tag}_gte")
        L = LTM[:, 0:16]
        W = LTM[:, 16:96]
        g = ["ffg"]

        def gop(emit, t, act=False):
            p.op("act" if act else "dve", emit, reads=g, writes=g, tag=f"{tag}_{t}")

        gop(lambda e: e.tensor_tensor(W[:, 0:4], L[:, 0::4], L[:, 1::4], AL.max), "mab")
        gop(lambda e: e.tensor_tensor(W[:, 4:8], L[:, 0::4], L[:, 1::4], AL.min), "nab")
        gop(lambda e: e.tensor_tensor(W[:, 8:12], L[:, 2::4], L[:, 3::4], AL.max), "mcd")
        gop(lambda e: e.tensor_tensor(W[:, 12:16], L[:, 2::4], L[:, 3::4], AL.min), "ncd")
        gop(lambda e: e.tensor_tensor(W[:, 16:20], W[:, 0:4], W[:, 8:12], AL.min), "t1")
        gop(lambda e: e.tensor_tensor(W[:, 20:24], W[:, 4:8], W[:, 12:16], AL.max), "t2")
        gop(lambda e: e.tensor_tensor(W[:, 24:28], W[:, 16:20], W[:, 20:24], AL.max), "m2")
        gop(lambda e: e.activation(W[:, 32:48], L[:], AF.Exp), "gexp", act=True)
        for ei in range(4):
            gop(lambda e, ei=ei: e.tensor_tensor(W[:, 48 + ei * 4:52 + ei * 4],
                                                 L[:, ei::4], W[:, 24:28], AL.is_ge), f"ge{ei}")
            gop(lambda e, ei=ei: e.tensor_tensor(W[:, 64 + ei * 4:68 + ei * 4],
                                                 W[:, 48 + ei * 4:52 + ei * 4],
                                                 W[:, 32 + ei:48:4], AL.mult), f"gm{ei}")
        gop(lambda e: e.tensor_tensor(W[:, 0:4], W[:, 64:68], W[:, 68:72], AL.add), "z1")
        gop(lambda e: e.tensor_tensor(W[:, 4:8], W[:, 72:76], W[:, 76:80], AL.add), "z2")
        gop(lambda e: e.tensor_tensor(W[:, 0:4], W[:, 0:4], W[:, 4:8], AL.add), "z")
        gop(lambda e: e.reciprocal(W[:, 4:8], W[:, 0:4]), "rz")
        for ei in range(4):
            gop(lambda e, ei=ei: e.tensor_tensor(W[:, 48 + ei * 4:52 + ei * 4],
                                                 W[:, 64 + ei * 4:68 + ei * 4],
                                                 W[:, 4:8], AL.mult), f"gn{ei}")
        for s in range(4):
            p.op("dve", lambda e, s=s: e.tensor_copy(GT8[:, s * 4:s * 4 + 4],
                                                     W[:, 48 + s:64:4]),
                 reads=g, writes=["ffgt8"], tag=f"{tag}_gc{s}")
            p.op("pe", lambda e, s=s: e.transpose(
                PB[0:4, s * 128:(s + 1) * 128], GT8[:, s * 4:s * 4 + 4], idb_sb[:]),
                 reads=["ffgt8", "idb"], writes=["PB"], tag=f"{tag}_gbt{s}")
        p.op("dve", lambda e: e.tensor_copy(GT[:], PB[0:4, 0:512]),
             reads=["PB"], writes=["ffgt"], tag=f"{tag}_gtev")
        p.op("pe", lambda e: e.matmul(P4[0:ER, :], e4_sb[:], GT[:], start=True, stop=True),
             reads=["ffgt", "e4"], writes=["P4"], tag=f"{tag}_g64")
        p.op("dve", lambda e: e.tensor_tensor(GM[:], MLT[0:ER, :], P4[0:ER, :], AL.mult),
             reads=["P4", "ffmlt"], writes=["ffgm"], tag=f"{tag}_gmid")
        for s in range(4):
            for oc in range(DC // 4):
                slp = (s * (DC // 4) + oc) % 2
                p.op("pe", lambda e, s=s, oc=oc, slp=slp: e.matmul(
                    PQ[:, slp * 512:(slp + 1) * 512], GM[:, s * 128:(s + 1) * 128],
                    bb_sb[:, oc * 512:(oc + 1) * 512], start=True, stop=True),
                     reads=["ffgm", "bb"], writes=[f"PQ{slp}"], tag=f"{tag}_d{s}_{oc}")
                p.op("dve", lambda e, s=s, oc=oc, slp=slp: e.tensor_tensor(
                    X[:, s, oc * 512:(oc + 1) * 512], X[:, s, oc * 512:(oc + 1) * 512],
                    PQ[:, slp * 512:(slp + 1) * 512], AL.add),
                     reads=[f"PQ{slp}", f"{xk}:{s}"], writes=[f"{xk}:{s}"],
                     tag=f"{tag}_r{s}_{oc}")

    ph1 = sb.mark()
    Xg = [sb.alloc(f"X{i}", [128, 4, D], F32) for i in range(2)]
    XCS1 = [sb.alloc(f"XCS1_{i}", [128, 2, D], BF16) for i in range(2)]
    TRASH = sb.alloc("TRASH", [128, D], BF16)
    STg = [sb.alloc(f"STg{i}", [128, 16], F32) for i in range(2)]
    XTr = sb.alloc("XTr", [128, 8, 512], BF16)
    MLT1 = sb.alloc("MLT1", [ERL, 512], F32)
    LTM1 = sb.alloc("LTM1", [128, 96], F32)
    GT8a = sb.alloc("GT8a", [128, 16], BF16)
    GT1 = sb.alloc("GT1", [4, 512], BF16)
    GM1 = sb.alloc("GM1", [ER, 512], BF16)

    x0r = x0.rearrange("(gg s p) d -> gg p s d", s=4, p=128)
    h1r = h1h.rearrange("(gg s p) d -> gg p s d", s=4, p=128)
    h2r = h2h.rearrange("(gg s p) d -> gg p s d", s=4, p=128)
    x1r = x1h.rearrange("(gg s p) d -> gg p s d", s=4, p=128)

    def load_group(g):
        bi = g % 2
        for st in range(4):
            p.op("syd", lambda e, g=g, bi=bi, st=st: e.dma_start(
                Xg[bi][:, st, :], x0r[g, :, st, :]),
                 writes=[f"x{bi}:{st}"], tag=f"1a_ldx{g}_{st}", ch=f"x{bi}:{st}")

    def stage_a(g):
        bi = g % 2
        emit_ln_cast(Xg[bi], XCS1[bi], TRASH, STg[bi], f"x{bi}", f"st{bi}",
                     h1r[g], f"h1h:{g}", f"1a_ln1g{g}", f"xcs{bi}")

    def stage_b(g):
        bi = g % 2
        X = Xg[bi]
        xk = f"x{bi}"
        emit_ffn(h1h, f"h1h:{g}", g * 512, X, xk, XTr, MLT1, LTM1, GT8a, GT1, GM1, f"f1g{g}")
        p.op("scd", lambda e, g=g, X=X: e.dma_start(x1r[g], X[:]),
             reads=[f"{xk}:{st}" for st in range(4)], writes=["x1h"],
             tag=f"1a_stx1g{g}", ch=f"{xk}:st")
        emit_ln_cast(X, XCS1[bi], TRASH, STg[bi], xk, f"st{bi}",
                     h2r[g], "h2h:0", f"1a_ln2g{g}", f"xcs{bi}")

    load_group(0)
    if GO > 1:
        load_group(1)
    stage_a(0)
    for g in range(GO):
        if g + 1 < GO:
            stage_a(g + 1)
        stage_b(g)
        if g + 2 < GO:
            load_group(g + 2)
        p.barrier()
    sb.reset(ph1)

    ph1b = sb.mark()
    H2T = sb.alloc("H2T", [128, DC, SGW], BF16)
    Wb = [sb.alloc(f"Wb{i}", [128, DC * 128], BF16) for i in range(2)]
    QS = [sb.alloc(f"QS{i}", [128, SGW], BF16) for i in range(2)]

    SGW_O = OWN
    for dc in range(DC):
        p.op("syd", lambda e, dc=dc: e.dma_start_transpose(
            H2T[:, dc, 0:SGW_O], h2h[0:SGW_O, dc * 128:(dc + 1) * 128]),
             reads=["h2h:0"], writes=[f"h2t:{dc}"], tag=f"1b_tld{dc}")
    tgts = [(wk, oc, kin[oc * 128:(oc + 1) * 128, :], "kvk") for oc in range(KVH)]
    tgts += [(wv, oc, vin[oc * 128:(oc + 1) * 128, :], "kvv") for oc in range(KVH)]
    tgts += [(wq, oc, qth[oc], "qh") for oc in range(HEADS)]
    RG = [[0, 1], [2, 3], [4, 5], [6, 7]]
    for ti, (wsrc, oc, dsth, wkind) in enumerate(tgts):
        wbi = ti % 2
        ntc = OWN // 512
        p.op("gpd", lambda e, wsrc=wsrc, oc=oc, wbi=wbi: e.dma_start(
            Wb[wbi].rearrange("p (dc q) -> p dc q", q=128),
            wsrc.rearrange("(dc p) q -> p dc q", p=128)[:, :, oc * 128:(oc + 1) * 128]),
             writes=[f"wb{wbi}"], tag=f"1b_lw{ti}", ch=f"wbl{wbi}")
        for tc in range(ntc):
            pso = tc % 2
            psk = "P4" if pso == 0 else "P5"
            PS = P4 if pso == 0 else P5
            for dc in range(DC):
                p.op("pe", lambda e, wbi=wbi, dc=dc, tc=tc, PS=PS: e.matmul(
                    PS[:], Wb[wbi][:, dc * 128:(dc + 1) * 128],
                    H2T[:, dc, tc * 512:(tc + 1) * 512],
                    start=(dc == 0), stop=(dc == DC - 1)),
                     reads=[f"wb{wbi}", f"h2t:{dc}"], writes=[psk],
                     tag=f"1b_mm{ti}_{tc}_{dc}")
            p.op("act", lambda e, tc=tc, wbi=wbi, PS=PS: e.copy(
                QS[wbi][:, tc * 512:(tc + 1) * 512], PS[:]),
                 reads=[psk], writes=[f"qs{wbi}:{tc}"], tag=f"1b_ev{ti}_{tc}")
        p.op("syd", lambda e, dsth=dsth, wbi=wbi, ntc=ntc: e.dma_start(
            dsth[:, 0:ntc * 512], QS[wbi][:, 0:ntc * 512]),
             reads=[f"qs{wbi}:{tc}" for tc in range(ntc)],
             writes=[{"kvk": "kin", "kvv": "vin"}.get(wkind, f"{wkind}:{oc}")],
             tag=f"1b_st{ti}", ch=f"qs{wbi}:0")
        if wkind == "kvk" and oc == KVH - 1:
            p.op("pool", lambda e: e.collective_compute(
                "AllGather", mybir.AluOpType.bypass, replica_groups=RG,
                ins=[kin.opt()], outs=[kout.opt()]),
                 reads=["kin"], writes=["kout"], tag="1b_cck")
        if wkind == "kvv" and oc == KVH - 1:
            p.op("pool", lambda e: e.collective_compute(
                "AllGather", mybir.AluOpType.bypass, replica_groups=RG,
                ins=[vin.opt()], outs=[vout.opt()]),
                 reads=["vin"], writes=["vout"], tag="1b_ccv")
    p.barrier()
    sb.reset(ph1b)

    ph2 = sb.mark()
    KTs = [sb.alloc(f"KT{i}", [128, SEQ], BF16) for i in range(2)]
    VTs = [sb.alloc(f"VT{i}", [128, KC, 128], BF16) for i in range(2)]
    QTs = [sb.alloc(f"QT{i}", [128, OWN], BF16) for i in range(2)]
    PTq = [sb.alloc(f"PT{i}", [128, 1024], BF16) for i in range(4)]
    RZf = [sb.alloc(f"RZf{i}", [128, 512], F32) for i in range(2)]
    OS = [sb.alloc(f"OS{i}", [128, 512], BF16) for i in range(2)]

    KPR = KC // 2

    def kv_src(kind, kv, r):
        return r * KVH * 128 + kv * 128

    def load_kv(kv):
        bi = kv % 2
        for r in range(2):
            p.op("syd", lambda e, kv=kv, bi=bi, r=r: e.dma_start(
                KTs[bi][:, r * OWN:(r + 1) * OWN],
                kout[kv_src("k", kv, r):kv_src("k", kv, r) + 128, :]),
                 reads=["kout"], writes=[f"kt{bi}"], tag=f"2_ldk{kv}_{r}")
        for kc in range(KC):
            r, kcl = kc // KPR, kc % KPR
            p.op("syd", lambda e, kv=kv, kc=kc, bi=bi, r=r, kcl=kcl: e.dma_start_transpose(
                VTs[bi][:, kc, :],
                vout[kv_src("v", kv, r) + 0:kv_src("v", kv, r) + 128,
                     kcl * 128:(kcl + 1) * 128]),
                 reads=["vout"], writes=[f"vt{bi}"], tag=f"2_ldv{kv}_{kc}")

    def load_q(h):
        bi = h % 2
        p.op("syd", lambda e, h=h, bi=bi: e.dma_start(QTs[bi][:], qth[h]),
             reads=[f"qh:{h}"], writes=[f"qt{bi}"], tag=f"2_ldq{h}")

    load_kv(0)
    load_q(0)
    for kv in range(KVH):
        kbi = kv % 2
        if kv + 1 < KVH:
            load_kv(kv + 1)
        KT, VT = KTs[kbi], VTs[kbi]
        for qh in range(QPK):
            h = kv * QPK + qh
            qbi = h % 2
            if h + 1 < HEADS:
                load_q(h + 1)
            QT = QTs[qbi]
            for qc in range(QC):
                okey = "P4" if qc % 2 == 0 else "P6"
                OP = P4 if qc % 2 == 0 else P6
                NPAIR = KC // 2

                def s_pair(pr):
                    rq = pr % 2
                    r = pr % 4
                    for i in range(2):
                        kc = pr * 2 + i
                        p.op("pe", lambda e, kc=kc, i=i, rq=rq, qc=qc, KT=KT, QT=QT: e.matmul(
                            PQ[:, rq * 1024 + i * 512:rq * 1024 + (i + 1) * 512],
                            KT[:, kc * 128:(kc + 1) * 128],
                            QT[:, qc * 512:(qc + 1) * 512], start=True, stop=True),
                             reads=[f"kt{kbi}", f"qt{qbi}"], writes=[f"PQr{rq}"],
                             tag=f"2_s{h}_{qc}_{kc}")
                    p.op("act", lambda e, r=r, rq=rq: e.activation(
                        PTq[r][:, 0:1024], PQ[:, rq * 1024:(rq + 1) * 1024],
                        AF.Exp, scale=SCL),
                         reads=[f"PQr{rq}"], writes=[f"pt{r}"], tag=f"2_e{h}_{qc}_{pr}")

                def pvz_pair(pr):
                    r = pr % 4
                    for i in range(2):
                        kc = pr * 2 + i
                        p.op("pe", lambda e, kc=kc, i=i, r=r, VT=VT, OP=OP: e.matmul(
                            OP[:], VT[:, kc, :], PTq[r][:, i * 512:(i + 1) * 512],
                            start=(kc == 0), stop=(kc == KC - 1)),
                             reads=[f"vt{kbi}", f"pt{r}"], writes=[okey],
                             tag=f"2_pv{h}_{qc}_{kc}")
                    for i in range(2):
                        kc = pr * 2 + i
                        p.op("pe", lambda e, kc=kc, i=i, r=r: e.matmul(
                            P5[:], ones_sb[:], PTq[r][:, i * 512:(i + 1) * 512],
                            start=(kc == 0), stop=(kc == KC - 1)),
                             reads=["ones", f"pt{r}"], writes=["P5"],
                             tag=f"2_z{h}_{qc}_{kc}")

                s_pair(0)
                s_pair(1)
                s_pair(2)
                for pr in range(3, NPAIR):
                    s_pair(pr)
                    pvz_pair(pr - 3)
                pvz_pair(NPAIR - 3)
                pvz_pair(NPAIR - 2)
                pvz_pair(NPAIR - 1)
                ob = (h * QC + qc) % 2
                p.op("dve", lambda e, ob=ob: e.reciprocal(RZf[ob][:], P5[:]),
                     reads=["P5"], writes=[f"rzf{ob}"], tag=f"2_rz{h}_{qc}")
                p.op("dve", lambda e, ob=ob, OP=OP: e.tensor_tensor(
                    OS[ob][:], OP[:], RZf[ob][:], AL.mult),
                     reads=[okey, f"rzf{ob}"], writes=[f"os{ob}"], tag=f"2_oe{h}_{qc}")
                p.op("syd", lambda e, h=h, qc=qc, ob=ob: e.dma_start(oh[qc, h], OS[ob][:]),
                     reads=[f"os{ob}"], writes=[f"oh:{qc}"], tag=f"2_ost{h}_{qc}", ch=f"os{ob}")
    p.barrier()
    sb.reset(ph2)

    ph3 = sb.mark()
    OQ = [sb.alloc(f"OQ{i}", [128, HEADS, 512], BF16) for i in range(2)]
    WOb = [sb.alloc(f"WOb{i}", [128, DC * 512], BF16) for i in range(2)]
    X1t = [sb.alloc(f"X1t{i}", [128, 512], F32) for i in range(2)]
    X2s = [sb.alloc(f"X2s{i}", [128, 512], F32) for i in range(2)]

    def load_wo(jc):
        wbi = jc % 2
        p.op("gpd", lambda e, jc=jc, wbi=wbi: e.dma_start(
            WOb[wbi].rearrange("p (dc q) -> p dc q", q=512),
            wo.rearrange("(dc p) q -> p dc q", p=128)[:, :, jc * 512:(jc + 1) * 512]),
             writes=[f"wob{wbi}"], tag=f"3_lw{jc}", ch=f"wobl{wbi}")

    def load_oq(jc, qc):
        bi = qc % 2
        p.op("syd", lambda e, qc=qc, bi=bi: e.dma_start(
            OQ[bi][:], oh[qc].rearrange("hh p q -> p hh q")),
             reads=[f"oh:{qc}"], writes=[f"oq{bi}"], tag=f"3_ldo{jc}_{qc}")

    load_wo(0)
    for jc in range(JCN):
        wbi = jc % 2
        if jc + 1 < JCN:
            load_wo(jc + 1)
        load_oq(jc, 0)
        for qci in range(QC):
            obi = qci % 2
            if qci + 1 < QC:
                load_oq(jc, qci + 1)
            for s in range(4):
                qt = qci * 4 + s
                pso = qt % 2
                psk = "P4" if pso == 0 else "P5"
                PS = P4 if pso == 0 else P5
                for hd in range(HEADS):
                    p.op("pe", lambda e, hd=hd, s=s, obi=obi, wbi=wbi, PS=PS: e.matmul(
                        PS[:], OQ[obi][:, hd, s * 128:(s + 1) * 128],
                        WOb[wbi][:, hd * 512:(hd + 1) * 512],
                        start=(hd == 0), stop=(hd == HEADS - 1)),
                         reads=[f"oq{obi}", f"wob{wbi}"], writes=[psk],
                         tag=f"3_mm{jc}_{qt}_{hd}")
                xb = qt % 2
                p.op("scd", lambda e, qt=qt, jc=jc, xb=xb: e.dma_start(
                    X1t[xb][:], x1h[qt * 128:(qt + 1) * 128, jc * 512:(jc + 1) * 512]),
                     reads=["x1h"], writes=[f"x1t{xb}"], tag=f"3_ldx1_{jc}_{qt}")
                p.op("dve", lambda e, xb=xb, PS=PS: e.tensor_tensor(
                    X2s[xb][:], X1t[xb][:], PS[:], AL.add),
                     reads=[f"x1t{xb}", psk], writes=[f"x2s{xb}"], tag=f"3_add{jc}_{qt}")
                p.op("syd", lambda e, qt=qt, jc=jc, xb=xb: e.dma_start(
                    x2h[qt * 128:(qt + 1) * 128, jc * 512:(jc + 1) * 512], X2s[xb][:]),
                     reads=[f"x2s{xb}"], writes=[f"x2h:{qt // 4}"], tag=f"3_stx2_{jc}_{qt}", ch=f"x2s{xb}")
    p.barrier()
    sb.reset(ph3)

    ph4 = sb.mark()
    Xg4 = [sb.alloc(f"X4_{i}", [128, 4, D], F32) for i in range(2)]
    XCS4 = [sb.alloc(f"XCS4_{i}", [128, 2, D], BF16) for i in range(2)]
    TRASH4 = sb.alloc("TRASH4", [128, D], BF16)
    STg4 = [sb.alloc(f"ST4{i}", [128, 16], F32) for i in range(2)]
    XTr4 = sb.alloc("XTr4", [128, 8, 512], BF16)
    MLT4 = sb.alloc("MLT4", [ERL, 512], F32)
    LTM4 = sb.alloc("LTM4", [128, 96], F32)
    GT8b = sb.alloc("GT8b", [128, 16], BF16)
    GT4 = sb.alloc("GT4", [4, 512], BF16)
    GM4 = sb.alloc("GM4", [ER, 512], BF16)
    x2r = x2h.rearrange("(gg s p) d -> gg p s d", s=4, p=128)
    h3r = h3h.rearrange("(gg s p) d -> gg p s d", s=4, p=128)
    outr = out.rearrange("(gg s p) d -> gg p s d", s=4, p=128)

    def load_group4(g):
        bi = g % 2
        for st in range(4):
            p.op("syd", lambda e, g=g, bi=bi, st=st: e.dma_start(
                Xg4[bi][:, st, :], x2r[g, :, st, :]),
                 reads=[f"x2h:{g}"], writes=[f"x4{bi}:{st}"], tag=f"4_ldx{g}_{st}",
                 ch=f"x4{bi}:{st}")

    def stage_a4(g):
        bi = g % 2
        emit_ln_cast(Xg4[bi], XCS4[bi], TRASH4, STg4[bi], f"x4{bi}", f"st4{bi}",
                     h3r[g], f"h3h:{g}", f"4_ln3g{g}", f"xcs4{bi}")

    def stage_b4(g):
        bi = g % 2
        X = Xg4[bi]
        xk = f"x4{bi}"
        emit_ffn(h3h, f"h3h:{g}", g * 512, X, xk, XTr4, MLT4, LTM4, GT8b, GT4, GM4, f"f2g{g}")
        p.op("syd", lambda e, g=g, X=X: e.dma_start(outr[g], X[:]),
             reads=[f"{xk}:{st}" for st in range(4)], tag=f"4_stout{g}", ch=f"{xk}:0")

    load_group4(0)
    if GO > 1:
        load_group4(1)
    stage_a4(0)
    for g in range(GO):
        if g + 1 < GO:
            stage_a4(g + 1)
        stage_b4(g)
        if g + 2 < GO:
            load_group4(g + 2)
        p.barrier()

    with nc.Block() as blk:
        p.materialize(blk)
    return nc, p


def _prep_host(x, wqn, wkn, wvn, won, A, B, Wr, cfg):
    At = np.ascontiguousarray(A.transpose(2, 0, 1).reshape(cfg["D"], 64))
    WrT = np.ascontiguousarray(Wr.T)
    AWr = np.ascontiguousarray(np.concatenate([At, WrT], axis=1))
    Bf = np.ascontiguousarray(B.transpose(0, 2, 1).reshape(64, cfg["D"]))
    return AWr, Bf


def kernel(L0_attn_x, L0_attn_wq, L0_attn_wk, L0_attn_wv, L0_attn_wo,
           L0_ffn_A, L0_ffn_B, L0_ffn_Wr):
    x = np.ascontiguousarray(np.asarray(L0_attn_x, dtype=np.float32))
    wqn = np.ascontiguousarray(np.asarray(L0_attn_wq, dtype=np.float32))
    wkn = np.ascontiguousarray(np.asarray(L0_attn_wk, dtype=np.float32))
    wvn = np.ascontiguousarray(np.asarray(L0_attn_wv, dtype=np.float32))
    won = np.ascontiguousarray(np.asarray(L0_attn_wo, dtype=np.float32))
    A = np.asarray(L0_ffn_A, dtype=np.float32)
    B = np.asarray(L0_ffn_B, dtype=np.float32)
    Wr = np.asarray(L0_ffn_Wr, dtype=np.float32)
    AWr, Bf = _prep_host(x, wqn, wkn, wvn, won, A, B, Wr, CFG_FULL)

    nc, _ = build_program(CFG_FULL)
    in_maps = []
    for c in range(8):
        b, h = c // 2, c % 2
        xb = x[b, h * 2048:(h + 1) * 2048]
        in_maps.append({"x0": np.ascontiguousarray(xb), "wq": wqn, "wk": wkn,
                        "wv": wvn, "wo": won, "awr": AWr, "bfl": Bf})
    res = run_bass_kernel_spmd(nc, in_maps, core_ids=list(range(8)))
    outp = np.empty((4, 4096, 4096), dtype=np.float32)
    for c in range(8):
        b, h = c // 2, c % 2
        outp[b, h * 2048:(h + 1) * 2048] = res.results[c]["out"]
    return outp


# version 24
# speedup vs baseline: 1.1557x; 1.0081x over previous
import math
import numpy as np
import ml_dtypes
import concourse.bass as bass
import concourse.mybir as mybir
from concourse.bass_utils import run_bass_kernel_spmd

F32 = mybir.dt.float32
BF16 = mybir.dt.bfloat16
AF = mybir.ActivationFunctionType
AL = mybir.AluOpType
AX = mybir.AxisListType.X
bf16 = ml_dtypes.bfloat16

QUEUES = ("pe", "act", "dve", "pool", "syd", "gpd", "scd")
ENGINE_OF = {"pe": "tensor", "act": "scalar", "dve": "vector", "pool": "gpsimd",
             "syd": "sync", "gpd": "gpsimd", "scd": "scalar"}
DMA_QUEUES = {"syd", "gpd", "scd"}


class Op:
    __slots__ = ("queue", "ch", "emit", "reads", "writes", "deps", "need_inc",
                 "inc_idx", "waits", "tag")

    def __init__(self, queue, ch, emit, reads, writes, tag):
        self.queue = queue
        self.ch = ch
        self.emit = emit
        self.reads = reads
        self.writes = writes
        self.deps = set()
        self.need_inc = False
        self.inc_idx = None
        self.waits = ()
        self.tag = tag


class Prog:

    def __init__(self, nc):
        self.nc = nc
        self.ops = []
        self.key_w = {}
        self.key_r = {}
        self.last_op = {}
        self.nsem = 0

    def op(self, queue, emit, reads=(), writes=(), tag="", ch=None):
        if ch is None:
            if queue in DMA_QUEUES:
                ch = writes[0] if writes else reads[0]
            else:
                ch = queue
        o = Op(queue, ch, emit, tuple(reads), tuple(writes), tag)
        for k in o.reads:
            for w in self.key_w.get(k, ()):
                o.deps.add(w)
        for k in o.writes:
            for w in self.key_w.get(k, ()):
                o.deps.add(w)
            for r in self.key_r.get(k, ()):
                o.deps.add(r)
        for k in o.writes:
            self.key_w[k] = [o]
            self.key_r[k] = []
        for k in o.reads:
            self.key_r.setdefault(k, []).append(o)
        o.deps.discard(o)
        self.ops.append(o)
        self.last_op[o.ch] = o
        return o

    def barrier(self):
        lasts = [op for op in self.last_op.values() if op is not None]
        for q in QUEUES:
            o = Op(q, q, (lambda e: e.nop()), (), (), "barrier")
            o.deps.update(lasts)
            self.ops.append(o)
            self.last_op[q] = o
        self.key_w.clear()
        self.key_r.clear()

    @staticmethod
    def _skip(d, o):
        return d.queue == "pe" and o.queue == "pe"

    def materialize(self, block):
        for o in self.ops:
            if o.queue in DMA_QUEUES or o.queue == "pool":
                o.need_inc = True
            for d in o.deps:
                if not self._skip(d, o):
                    d.need_inc = True
        cnt = {}
        for o in self.ops:
            if o.need_inc:
                cnt[o.ch] = cnt.get(o.ch, 0) + 1
                o.inc_idx = cnt[o.ch]
        sems = {c: self.nc.alloc_semaphore(f"s{i}")
                for i, c in enumerate(sorted(cnt))}
        assert len(sems) <= 100, f"too many sem channels: {len(sems)}"
        dma_ch = {o.ch for o in self.ops if o.queue in DMA_QUEUES}
        clock = {}
        snap = {}
        for o in self.ops:
            oc = clock.setdefault(o.ch, {})
            needed = {}
            for d in o.deps:
                if self._skip(d, o):
                    continue
                assert d.inc_idx is not None, (d.tag, o.tag)
                if oc.get(d.ch, 0) < d.inc_idx:
                    needed[d.ch] = max(needed.get(d.ch, 0), d.inc_idx)
            o.waits = tuple(sorted(needed.items()))
            for pq, v in o.waits:
                for k2, v2 in snap[(pq, v)].items():
                    if oc.get(k2, 0) < v2:
                        oc[k2] = v2
                if oc.get(pq, 0) < v:
                    oc[pq] = v
            if o.need_inc:
                sn = dict(oc)
                sn[o.ch] = o.inc_idx
                snap[(o.ch, o.inc_idx)] = sn
        by_engine = {}
        for o in self.ops:
            by_engine.setdefault(ENGINE_OF[o.queue], []).append(o)
        for engine, ops in by_engine.items():
            def body(e, ops=ops):
                for o in ops:
                    for pq, v in o.waits:
                        e.wait_ge(sems[pq], v * (16 if pq in dma_ch else 1))
                    inst = o.emit(e)
                    if o.need_inc:
                        inst.then_inc(sems[o.ch],
                                      16 if o.ch in dma_ch else 1)
            getattr(block, engine)(body)


class SbufAlloc:
    def __init__(self, nc):
        self.nc = nc
        self.off = 16640
        self.n = 0

    def alloc(self, name, shape, dtype):
        t = self.nc.alloc_sbuf_tensor_at(f"{name}_{self.n}", list(shape), dtype,
                                         offset=self.off)
        self.n += 1
        per_part = int(np.prod(shape[1:])) * mybir.dt.size(dtype)
        self.off += (per_part + 31) & ~31
        assert self.off <= 229376 - 64, f"SBUF overflow at {name}: {self.off}"
        return t

    def mark(self):
        return self.off

    def reset(self, m):
        self.off = m


CFG_FULL = dict(D=4096, SEQ=4096, OWN=2048, HEADS=32, KVH=8)
CFG_SMALL = dict(D=512, SEQ=1024, OWN=512, HEADS=4, KVH=1)


def build_program(cfg):
    D = cfg["D"]; SEQ = cfg["SEQ"]; OWN = cfg["OWN"]
    HEADS = cfg["HEADS"]; KVH = cfg["KVH"]
    HD = 128
    NEXP, LORA = 4, 16
    DC = D // 128
    ER = NEXP * LORA
    ERL = ER + NEXP
    GF = SEQ // 512
    GO = OWN // 512
    QPK = HEADS // KVH
    KC = SEQ // 128
    NQUAD = (KC + 3) // 4
    DKV = KVH * HD
    JCN = D // 512
    SCL = 1.0 / math.sqrt(HD)
    NSG = max(1, SEQ // 2048)
    SGW = SEQ // NSG
    TCN = SGW // 512
    QC = OWN // 512

    nc = bass.Bass("TRN2", target_bir_lowering=False)

    x0 = nc.dram_tensor("x0", [OWN, D], F32, kind="ExternalInput").ap()
    wq = nc.dram_tensor("wq", [D, D], F32, kind="ExternalInput").ap()
    wk = nc.dram_tensor("wk", [D, DKV], F32, kind="ExternalInput").ap()
    wv = nc.dram_tensor("wv", [D, DKV], F32, kind="ExternalInput").ap()
    wo = nc.dram_tensor("wo", [D, D], F32, kind="ExternalInput").ap()
    awr = nc.dram_tensor("awr", [D, ERL], F32, kind="ExternalInput").ap()
    bfl = nc.dram_tensor("bfl", [ER, D], F32, kind="ExternalInput").ap()
    out = nc.dram_tensor("out", [OWN, D], F32, kind="ExternalOutput").ap()

    h1h = nc.dram_tensor("h1h", [OWN, D], BF16, kind="Internal").ap()
    h2h = nc.dram_tensor("h2h", [OWN, D], BF16, kind="Internal").ap()
    h3h = nc.dram_tensor("h3h", [OWN, D], BF16, kind="Internal").ap()
    x1h = nc.dram_tensor("x1h", [OWN, D], F32, kind="Internal").ap()
    x2h = nc.dram_tensor("x2h", [OWN, D], F32, kind="Internal").ap()
    qth = nc.dram_tensor("qth", [HEADS, 128, OWN], BF16, kind="Internal").ap()
    kin = nc.dram_tensor("kin", [KVH * 128, OWN], BF16, kind="Internal").ap()
    vin = nc.dram_tensor("vin", [KVH * 128, OWN], BF16, kind="Internal").ap()
    kout = nc.dram_tensor("kout", [2 * KVH * 128, OWN], BF16, kind="Internal").ap()
    vout = nc.dram_tensor("vout", [2 * KVH * 128, OWN], BF16, kind="Internal").ap()
    oh = nc.dram_tensor("oh", [QC, HEADS, 128, 512], BF16, kind="Internal").ap()

    onesc = nc.inline_tensor(np.ones((128, 128), dtype=bf16), name="c_ones").ap()
    idbc = nc.inline_tensor(np.eye(128, dtype=bf16), name="c_idb").ap()
    idfc = nc.inline_tensor(np.eye(128, dtype=np.float32), name="c_idf").ap()
    E4np = np.zeros((NEXP, ER), dtype=bf16)
    for e_ in range(NEXP):
        E4np[e_, e_ * LORA:(e_ + 1) * LORA] = 1
    e4c = nc.inline_tensor(E4np, name="c_e4").ap()

    PQ = nc.alloc_psum_tensor("PQ", [128, 2048], F32)
    P4 = nc.alloc_psum_tensor("P4", [128, 512], F32)
    P5 = nc.alloc_psum_tensor("P5", [128, 512], F32)
    P6 = nc.alloc_psum_tensor("P6", [128, 512], F32)
    PB = nc.alloc_psum_tensor("PB", [128, 1024], BF16)

    p = Prog(nc)
    sb = SbufAlloc(nc)

    CZ = sb.alloc("cz", [128, 4], F32)
    CZB = sb.alloc("czb", [128, 4], BF16)
    p.op("dve", lambda e: e.memset(CZ[:, 0:1], 0.0), writes=["cz"], tag="cz0")
    p.op("dve", lambda e: e.memset(CZ[:, 1:2], 1.0), writes=["cz"], tag="cz1")
    p.op("dve", lambda e: e.memset(CZB[:, 0:1], 0.0), writes=["cz"], tag="czb0")
    p.op("dve", lambda e: e.memset(CZB[:, 1:2], 1.0), writes=["cz"], tag="czb1")
    nc.const_aps.register(F32, 0.0, CZ[:, 0:1])
    nc.const_aps.register(F32, 1.0, CZ[:, 1:2])
    nc.const_aps.register(BF16, 0.0, CZB[:, 0:1])
    nc.const_aps.register(BF16, 1.0, CZB[:, 1:2])

    ones_sb = sb.alloc("ones", [128, 128], BF16)
    idb_sb = sb.alloc("idb", [128, 128], BF16)
    idf_sb = sb.alloc("idf", [128, 128], F32)
    e4_sb = sb.alloc("e4", [NEXP, ER], BF16)
    atw_sb = sb.alloc("atw", [128, DC * ERL], BF16)
    bb_sb = sb.alloc("bb", [ER, D], BF16)

    ph0 = sb.mark()
    atwf = sb.alloc("atwf", [128, DC * ERL], F32)
    bf_f = sb.alloc("bf_f", [ER, D], F32)
    p.op("syd", lambda e: e.dma_start(ones_sb[:], onesc), writes=["ones"], tag="ld_ones")
    p.op("syd", lambda e: e.dma_start(idb_sb[:], idbc), writes=["idb"], tag="ld_idb")
    p.op("syd", lambda e: e.dma_start(idf_sb[:], idfc), writes=["idf"], tag="ld_idf")
    p.op("syd", lambda e: e.dma_start(e4_sb[:], e4c), writes=["e4"], tag="ld_e4")
    p.op("syd", lambda e: e.dma_start(
        atwf.rearrange("p (dc q) -> p dc q", q=ERL),
        awr.rearrange("(dc p) q -> p dc q", p=128)), writes=["atwf"], tag="ld_awr")
    p.op("syd", lambda e: e.dma_start(bf_f[:], bfl), writes=["bf_f"], tag="ld_bf")
    p.op("dve", lambda e: e.tensor_copy(atw_sb[:], atwf[:]),
         reads=["atwf"], writes=["atw"], tag="cast_atw")
    p.op("act", lambda e: e.mul(bb_sb[:], bf_f[:], 2.0),
         reads=["bf_f"], writes=["bb"], tag="cast_bb")
    p.barrier()
    sb.reset(ph0)

    def emit_ln_cast(X, XCS, TRASH, ST, xk, stk, hrow, hkey, tag, xcpfx):
        for st_ in range(4):
            p.op("dve", lambda e, st_=st_: e.reduce_sum(ST[:, st_:st_ + 1], X[:, st_, :], axis=AX),
                 reads=[f"{xk}:{st_}"], writes=[f"{stk}:a"], tag=f"{tag}_sum{st_}")
            p.op("act", lambda e, st_=st_: e.activation(TRASH[:], X[:, st_, :], AF.Square,
                                                        accum_out=ST[:, 4 + st_:5 + st_]),
                 reads=[f"{xk}:{st_}"], writes=["lntrash", f"{stk}:b"], tag=f"{tag}_sq{st_}")
        p.op("dve", lambda e: e.tensor_scalar(ST[:, 8:12], ST[:, 0:4], 1.0 / D, None, AL.mult),
             reads=[f"{stk}:a"], writes=[f"{stk}:m"], tag=f"{tag}_m")
        p.op("dve", lambda e: e.tensor_scalar(ST[:, 4:8], ST[:, 4:8], 1.0 / D, None, AL.mult),
             reads=[f"{stk}:b"], writes=[f"{stk}:b"], tag=f"{tag}_ex2")
        p.op("dve", lambda e: e.tensor_tensor(ST[:, 12:16], ST[:, 8:12], ST[:, 8:12], AL.mult),
             reads=[f"{stk}:m"], writes=[f"{stk}:t"], tag=f"{tag}_m2")
        p.op("dve", lambda e: e.tensor_tensor(ST[:, 4:8], ST[:, 4:8], ST[:, 12:16], AL.subtract),
             reads=[f"{stk}:b", f"{stk}:t"], writes=[f"{stk}:b"], tag=f"{tag}_var")
        p.op("dve", lambda e: e.tensor_scalar(ST[:, 4:8], ST[:, 4:8], 1e-5, None, AL.add),
             reads=[f"{stk}:b"], writes=[f"{stk}:b"], tag=f"{tag}_eps")
        p.op("act", lambda e: e.activation(ST[:, 4:8], ST[:, 4:8], AF.Ln),
             reads=[f"{stk}:b"], writes=[f"{stk}:b"], tag=f"{tag}_ln")
        p.op("act", lambda e: e.activation(ST[:, 4:8], ST[:, 4:8], AF.Exp, scale=-0.5),
             reads=[f"{stk}:b"], writes=[f"{stk}:b"], tag=f"{tag}_rstd")
        p.op("dve", lambda e: e.tensor_tensor(ST[:, 12:16], ST[:, 8:12], ST[:, 4:8], AL.mult),
             reads=[f"{stk}:b", f"{stk}:m"], writes=[f"{stk}:t"], tag=f"{tag}_mr")
        p.op("dve", lambda e: e.tensor_scalar(ST[:, 12:16], ST[:, 12:16], -1.0, None, AL.mult),
             reads=[f"{stk}:t"], writes=[f"{stk}:t"], tag=f"{tag}_nmr")
        xckey = xcpfx
        for st_ in range(4):
            sl = st_ % 2
            if st_ < 2:
                p.op("act", lambda e, st_=st_, sl=sl: e.activation(
                    XCS[:, sl, :], X[:, st_, :], AF.Identity,
                    bias=ST[:, 12 + st_:13 + st_], scale=ST[:, 4 + st_:5 + st_]),
                     reads=[f"{xk}:{st_}", f"{stk}:b", f"{stk}:t"], writes=[f"{xckey}:{sl}"],
                     tag=f"{tag}_cast{st_}")
            else:
                p.op("dve", lambda e, st_=st_, sl=sl: e.tensor_scalar(
                    XCS[:, sl, :], X[:, st_, :], ST[:, 4 + st_:5 + st_],
                    ST[:, 12 + st_:13 + st_], AL.mult, AL.add),
                     reads=[f"{xk}:{st_}", f"{stk}:b", f"{stk}:t"], writes=[f"{xckey}:{sl}"],
                     tag=f"{tag}_cast{st_}")
            p.op("scd", lambda e, st_=st_, sl=sl: e.dma_start(hrow[:, st_, :], XCS[:, sl, :]),
                 reads=[f"{xckey}:{sl}"], writes=[hkey], tag=f"{tag}_hst{st_}", ch=f"{xckey}:{sl}")

    def emit_ffn(hh, hkey, grow, X, xk, XT, MLT, LTM, GT8, GT, GM, tag):
        for dc in range(DC):
            sl = dc % 8
            p.op("syd",
                 lambda e, dc=dc, sl=sl: e.dma_start_transpose(
                     XT[:, sl, :], hh[grow:grow + 512, dc * 128:(dc + 1) * 128]),
                 reads=[hkey], writes=[f"ffxt:{sl}"], tag=f"{tag}_tld{dc}")
            p.op("pe", lambda e, dc=dc, sl=sl: e.matmul(
                P4[0:ERL, :], atw_sb[:, dc * ERL:(dc + 1) * ERL], XT[:, sl, :],
                start=(dc == 0), stop=(dc == DC - 1)),
                 reads=[f"ffxt:{sl}", "atw"], writes=["P4"], tag=f"{tag}_mid{dc}")
        p.op("dve", lambda e: e.tensor_copy(MLT[:], P4[0:ERL, :]),
             reads=["P4"], writes=["ffmlt"], tag=f"{tag}_mide")
        for s in range(4):
            p.op("pe", lambda e, s=s: e.transpose(
                PQ[0:128, s * 4:s * 4 + 4], MLT[ER:ERL, s * 128:(s + 1) * 128], idf_sb[ER:ERL, ER:ERL]),
                 reads=["ffmlt", "idf"], writes=["PQ0"], tag=f"{tag}_gtr{s}")
        p.op("dve", lambda e: e.tensor_copy(LTM[:, 0:16], PQ[:, 0:16]),
             reads=["PQ0"], writes=["ffg"], tag=f"{tag}_gte")
        L = LTM[:, 0:16]
        W = LTM[:, 16:96]
        g = ["ffg"]

        def gop(emit, t, act=False):
            p.op("act" if act else "dve", emit, reads=g, writes=g, tag=f"{tag}_{t}")

        gop(lambda e: e.tensor_tensor(W[:, 0:4], L[:, 0::4], L[:, 1::4], AL.max), "mab")
        gop(lambda e: e.tensor_tensor(W[:, 4:8], L[:, 0::4], L[:, 1::4], AL.min), "nab")
        gop(lambda e: e.tensor_tensor(W[:, 8:12], L[:, 2::4], L[:, 3::4], AL.max), "mcd")
        gop(lambda e: e.tensor_tensor(W[:, 12:16], L[:, 2::4], L[:, 3::4], AL.min), "ncd")
        gop(lambda e: e.tensor_tensor(W[:, 16:20], W[:, 0:4], W[:, 8:12], AL.min), "t1")
        gop(lambda e: e.tensor_tensor(W[:, 20:24], W[:, 4:8], W[:, 12:16], AL.max), "t2")
        gop(lambda e: e.tensor_tensor(W[:, 24:28], W[:, 16:20], W[:, 20:24], AL.max), "m2")
        gop(lambda e: e.activation(W[:, 32:48], L[:], AF.Exp), "gexp", act=True)
        for ei in range(4):
            gop(lambda e, ei=ei: e.tensor_tensor(W[:, 48 + ei * 4:52 + ei * 4],
                                                 L[:, ei::4], W[:, 24:28], AL.is_ge), f"ge{ei}")
            gop(lambda e, ei=ei: e.tensor_tensor(W[:, 64 + ei * 4:68 + ei * 4],
                                                 W[:, 48 + ei * 4:52 + ei * 4],
                                                 W[:, 32 + ei:48:4], AL.mult), f"gm{ei}")
        gop(lambda e: e.tensor_tensor(W[:, 0:4], W[:, 64:68], W[:, 68:72], AL.add), "z1")
        gop(lambda e: e.tensor_tensor(W[:, 4:8], W[:, 72:76], W[:, 76:80], AL.add), "z2")
        gop(lambda e: e.tensor_tensor(W[:, 0:4], W[:, 0:4], W[:, 4:8], AL.add), "z")
        gop(lambda e: e.reciprocal(W[:, 4:8], W[:, 0:4]), "rz")
        for ei in range(4):
            gop(lambda e, ei=ei: e.tensor_tensor(W[:, 48 + ei * 4:52 + ei * 4],
                                                 W[:, 64 + ei * 4:68 + ei * 4],
                                                 W[:, 4:8], AL.mult), f"gn{ei}")
        for s in range(4):
            p.op("dve", lambda e, s=s: e.tensor_copy(GT8[:, s * 4:s * 4 + 4],
                                                     W[:, 48 + s:64:4]),
                 reads=g, writes=["ffgt8"], tag=f"{tag}_gc{s}")
            p.op("pe", lambda e, s=s: e.transpose(
                PB[0:4, s * 128:(s + 1) * 128], GT8[:, s * 4:s * 4 + 4], idb_sb[:]),
                 reads=["ffgt8", "idb"], writes=["PB"], tag=f"{tag}_gbt{s}")
        p.op("dve", lambda e: e.tensor_copy(GT[:], PB[0:4, 0:512]),
             reads=["PB"], writes=["ffgt"], tag=f"{tag}_gtev")
        p.op("pe", lambda e: e.matmul(P4[0:ER, :], e4_sb[:], GT[:], start=True, stop=True),
             reads=["ffgt", "e4"], writes=["P4"], tag=f"{tag}_g64")
        p.op("dve", lambda e: e.tensor_tensor(GM[:], MLT[0:ER, :], P4[0:ER, :], AL.mult),
             reads=["P4", "ffmlt"], writes=["ffgm"], tag=f"{tag}_gmid")
        for s in range(4):
            for oc in range(DC // 4):
                slp = (s * (DC // 4) + oc) % 2
                p.op("pe", lambda e, s=s, oc=oc, slp=slp: e.matmul(
                    PQ[:, slp * 512:(slp + 1) * 512], GM[:, s * 128:(s + 1) * 128],
                    bb_sb[:, oc * 512:(oc + 1) * 512], start=True, stop=True),
                     reads=["ffgm", "bb"], writes=[f"PQ{slp}"], tag=f"{tag}_d{s}_{oc}")
                p.op("dve", lambda e, s=s, oc=oc, slp=slp: e.tensor_tensor(
                    X[:, s, oc * 512:(oc + 1) * 512], X[:, s, oc * 512:(oc + 1) * 512],
                    PQ[:, slp * 512:(slp + 1) * 512], AL.add),
                     reads=[f"PQ{slp}", f"{xk}:{s}"], writes=[f"{xk}:{s}"],
                     tag=f"{tag}_r{s}_{oc}")

    ph1 = sb.mark()
    Xg = [sb.alloc(f"X{i}", [128, 4, D], F32) for i in range(2)]
    XCS1 = [sb.alloc(f"XCS1_{i}", [128, 2, D], BF16) for i in range(2)]
    TRASH = sb.alloc("TRASH", [128, D], BF16)
    STg = [sb.alloc(f"STg{i}", [128, 16], F32) for i in range(2)]
    XTr = sb.alloc("XTr", [128, 8, 512], BF16)
    MLT1 = sb.alloc("MLT1", [ERL, 512], F32)
    LTM1 = sb.alloc("LTM1", [128, 96], F32)
    GT8a = sb.alloc("GT8a", [128, 16], BF16)
    GT1 = sb.alloc("GT1", [4, 512], BF16)
    GM1 = sb.alloc("GM1", [ER, 512], BF16)

    x0r = x0.rearrange("(gg s p) d -> gg p s d", s=4, p=128)
    h1r = h1h.rearrange("(gg s p) d -> gg p s d", s=4, p=128)
    h2r = h2h.rearrange("(gg s p) d -> gg p s d", s=4, p=128)
    x1r = x1h.rearrange("(gg s p) d -> gg p s d", s=4, p=128)

    def load_group(g):
        bi = g % 2
        for st in range(4):
            p.op("syd", lambda e, g=g, bi=bi, st=st: e.dma_start(
                Xg[bi][:, st, :], x0r[g, :, st, :]),
                 writes=[f"x{bi}:{st}"], tag=f"1a_ldx{g}_{st}", ch=f"x{bi}:{st}")

    def stage_a(g):
        bi = g % 2
        emit_ln_cast(Xg[bi], XCS1[bi], TRASH, STg[bi], f"x{bi}", f"st{bi}",
                     h1r[g], f"h1h:{g}", f"1a_ln1g{g}", f"xcs{bi}")

    def stage_b(g):
        bi = g % 2
        X = Xg[bi]
        xk = f"x{bi}"
        emit_ffn(h1h, f"h1h:{g}", g * 512, X, xk, XTr, MLT1, LTM1, GT8a, GT1, GM1, f"f1g{g}")
        p.op("scd", lambda e, g=g, X=X: e.dma_start(x1r[g], X[:]),
             reads=[f"{xk}:{st}" for st in range(4)], writes=["x1h"],
             tag=f"1a_stx1g{g}", ch=f"{xk}:st")
        emit_ln_cast(X, XCS1[bi], TRASH, STg[bi], xk, f"st{bi}",
                     h2r[g], "h2h:0", f"1a_ln2g{g}", f"xcs{bi}")

    load_group(0)
    if GO > 1:
        load_group(1)
    stage_a(0)
    for g in range(GO):
        if g + 1 < GO:
            stage_a(g + 1)
        stage_b(g)
        if g + 2 < GO:
            load_group(g + 2)
        p.barrier()
    sb.reset(ph1)

    ph1b = sb.mark()
    H2T = sb.alloc("H2T", [128, DC, SGW], BF16)
    Wb = [sb.alloc(f"Wb{i}", [128, DC * 128], BF16) for i in range(4)]
    QS = [sb.alloc(f"QS{i}", [128, SGW], BF16) for i in range(2)]

    SGW_O = OWN
    for dc in range(DC):
        p.op("syd", lambda e, dc=dc: e.dma_start_transpose(
            H2T[:, dc, 0:SGW_O], h2h[0:SGW_O, dc * 128:(dc + 1) * 128]),
             reads=["h2h:0"], writes=[f"h2t:{dc}", f"h2tc:{dc % 8}"],
             tag=f"1b_tld{dc}", ch=f"h2tc:{dc % 8}")
    tgts = [(wk, oc, kin[oc * 128:(oc + 1) * 128, :], "kvk") for oc in range(KVH)]
    tgts += [(wv, oc, vin[oc * 128:(oc + 1) * 128, :], "kvv") for oc in range(KVH)]
    tgts += [(wq, oc, qth[oc], "qh") for oc in range(HEADS)]
    RG = [[0, 1], [2, 3], [4, 5], [6, 7]]
    for ti, (wsrc, oc, dsth, wkind) in enumerate(tgts):
        wbi = ti % 4
        qbi = ti % 2
        ntc = OWN // 512
        p.op("gpd", lambda e, wsrc=wsrc, oc=oc, wbi=wbi: e.dma_start(
            Wb[wbi].rearrange("p (dc q) -> p dc q", q=128),
            wsrc.rearrange("(dc p) q -> p dc q", p=128)[:, :, oc * 128:(oc + 1) * 128]),
             writes=[f"wb{wbi}"], tag=f"1b_lw{ti}", ch=f"wbl{wbi}")
        for tc in range(ntc):
            pso = tc % 2
            psk = "P4" if pso == 0 else "P5"
            PS = P4 if pso == 0 else P5
            for dc in range(DC):
                p.op("pe", lambda e, wbi=wbi, dc=dc, tc=tc, PS=PS: e.matmul(
                    PS[:], Wb[wbi][:, dc * 128:(dc + 1) * 128],
                    H2T[:, dc, tc * 512:(tc + 1) * 512],
                    start=(dc == 0), stop=(dc == DC - 1)),
                     reads=[f"wb{wbi}", f"h2t:{dc}"], writes=[psk],
                     tag=f"1b_mm{ti}_{tc}_{dc}")
            p.op("act", lambda e, tc=tc, qbi=qbi, PS=PS: e.copy(
                QS[qbi][:, tc * 512:(tc + 1) * 512], PS[:]),
                 reads=[psk], writes=[f"qs{qbi}:{tc}"], tag=f"1b_ev{ti}_{tc}")
        p.op("syd", lambda e, dsth=dsth, qbi=qbi, ntc=ntc: e.dma_start(
            dsth[:, 0:ntc * 512], QS[qbi][:, 0:ntc * 512]),
             reads=[f"qs{qbi}:{tc}" for tc in range(ntc)],
             writes=[{"kvk": "kin", "kvv": "vin"}.get(wkind, f"{wkind}:{oc}")],
             tag=f"1b_st{ti}", ch=f"qs{qbi}:0")
        if wkind == "kvk" and oc == KVH - 1:
            p.op("pool", lambda e: e.collective_compute(
                "AllGather", mybir.AluOpType.bypass, replica_groups=RG,
                ins=[kin.opt()], outs=[kout.opt()]),
                 reads=["kin"], writes=["kout"], tag="1b_cck")
        if wkind == "kvv" and oc == KVH - 1:
            p.op("pool", lambda e: e.collective_compute(
                "AllGather", mybir.AluOpType.bypass, replica_groups=RG,
                ins=[vin.opt()], outs=[vout.opt()]),
                 reads=["vin"], writes=["vout"], tag="1b_ccv")
    p.barrier()
    sb.reset(ph1b)

    ph2 = sb.mark()
    KTs = [sb.alloc(f"KT{i}", [128, SEQ], BF16) for i in range(2)]
    VTs = [sb.alloc(f"VT{i}", [128, KC, 128], BF16) for i in range(2)]
    QTs = [sb.alloc(f"QT{i}", [128, OWN], BF16) for i in range(2)]
    PTq = [sb.alloc(f"PT{i}", [128, 1024], BF16) for i in range(4)]
    RZf = [sb.alloc(f"RZf{i}", [128, 512], F32) for i in range(2)]
    OS = [sb.alloc(f"OS{i}", [128, 512], BF16) for i in range(2)]

    KPR = KC // 2

    def kv_src(kind, kv, r):
        return r * KVH * 128 + kv * 128

    def load_kv(kv):
        bi = kv % 2
        for r in range(2):
            p.op("syd", lambda e, kv=kv, bi=bi, r=r: e.dma_start(
                KTs[bi][:, r * OWN:(r + 1) * OWN],
                kout[kv_src("k", kv, r):kv_src("k", kv, r) + 128, :]),
                 reads=["kout"], writes=[f"kt{bi}"], tag=f"2_ldk{kv}_{r}")
        for kc in range(KC):
            r, kcl = kc // KPR, kc % KPR
            p.op("syd", lambda e, kv=kv, kc=kc, bi=bi, r=r, kcl=kcl: e.dma_start_transpose(
                VTs[bi][:, kc, :],
                vout[kv_src("v", kv, r) + 0:kv_src("v", kv, r) + 128,
                     kcl * 128:(kcl + 1) * 128]),
                 reads=["vout"], writes=[f"vt{bi}"], tag=f"2_ldv{kv}_{kc}")

    def load_q(h):
        bi = h % 2
        p.op("syd", lambda e, h=h, bi=bi: e.dma_start(QTs[bi][:], qth[h]),
             reads=[f"qh:{h}"], writes=[f"qt{bi}"], tag=f"2_ldq{h}")

    load_kv(0)
    load_q(0)
    for kv in range(KVH):
        kbi = kv % 2
        if kv + 1 < KVH:
            load_kv(kv + 1)
        KT, VT = KTs[kbi], VTs[kbi]
        for qh in range(QPK):
            h = kv * QPK + qh
            qbi = h % 2
            if h + 1 < HEADS:
                load_q(h + 1)
            QT = QTs[qbi]
            for qc in range(QC):
                okey = "P4" if qc % 2 == 0 else "P6"
                OP = P4 if qc % 2 == 0 else P6
                NPAIR = KC // 2

                def s_pair(pr):
                    rq = pr % 2
                    r = pr % 4
                    for i in range(2):
                        kc = pr * 2 + i
                        p.op("pe", lambda e, kc=kc, i=i, rq=rq, qc=qc, KT=KT, QT=QT: e.matmul(
                            PQ[:, rq * 1024 + i * 512:rq * 1024 + (i + 1) * 512],
                            KT[:, kc * 128:(kc + 1) * 128],
                            QT[:, qc * 512:(qc + 1) * 512], start=True, stop=True),
                             reads=[f"kt{kbi}", f"qt{qbi}"], writes=[f"PQr{rq}"],
                             tag=f"2_s{h}_{qc}_{kc}")
                    p.op("act", lambda e, r=r, rq=rq: e.activation(
                        PTq[r][:, 0:1024], PQ[:, rq * 1024:(rq + 1) * 1024],
                        AF.Exp, scale=SCL),
                         reads=[f"PQr{rq}"], writes=[f"pt{r}"], tag=f"2_e{h}_{qc}_{pr}")

                def pvz_pair(pr):
                    r = pr % 4
                    for i in range(2):
                        kc = pr * 2 + i
                        p.op("pe", lambda e, kc=kc, i=i, r=r, VT=VT, OP=OP: e.matmul(
                            OP[:], VT[:, kc, :], PTq[r][:, i * 512:(i + 1) * 512],
                            start=(kc == 0), stop=(kc == KC - 1)),
                             reads=[f"vt{kbi}", f"pt{r}"], writes=[okey],
                             tag=f"2_pv{h}_{qc}_{kc}")
                    for i in range(2):
                        kc = pr * 2 + i
                        p.op("pe", lambda e, kc=kc, i=i, r=r: e.matmul(
                            P5[:], ones_sb[:], PTq[r][:, i * 512:(i + 1) * 512],
                            start=(kc == 0), stop=(kc == KC - 1)),
                             reads=["ones", f"pt{r}"], writes=["P5"],
                             tag=f"2_z{h}_{qc}_{kc}")

                s_pair(0)
                s_pair(1)
                s_pair(2)
                for pr in range(3, NPAIR):
                    s_pair(pr)
                    pvz_pair(pr - 3)
                pvz_pair(NPAIR - 3)
                pvz_pair(NPAIR - 2)
                pvz_pair(NPAIR - 1)
                ob = (h * QC + qc) % 2
                p.op("dve", lambda e, ob=ob: e.reciprocal(RZf[ob][:], P5[:]),
                     reads=["P5"], writes=[f"rzf{ob}"], tag=f"2_rz{h}_{qc}")
                p.op("dve", lambda e, ob=ob, OP=OP: e.tensor_tensor(
                    OS[ob][:], OP[:], RZf[ob][:], AL.mult),
                     reads=[okey, f"rzf{ob}"], writes=[f"os{ob}"], tag=f"2_oe{h}_{qc}")
                p.op("syd", lambda e, h=h, qc=qc, ob=ob: e.dma_start(oh[qc, h], OS[ob][:]),
                     reads=[f"os{ob}"], writes=[f"oh:{qc}"], tag=f"2_ost{h}_{qc}", ch=f"os{ob}")
    p.barrier()
    sb.reset(ph2)

    ph3 = sb.mark()
    OQ = [sb.alloc(f"OQ{i}", [128, HEADS, 512], BF16) for i in range(2)]
    WOb = [sb.alloc(f"WOb{i}", [128, DC * 512], BF16) for i in range(3)]
    X1t = [sb.alloc(f"X1t{i}", [128, 512], F32) for i in range(2)]
    X2s = [sb.alloc(f"X2s{i}", [128, 512], F32) for i in range(2)]

    def load_wo(jc):
        wbi = jc % 3
        p.op("gpd", lambda e, jc=jc, wbi=wbi: e.dma_start(
            WOb[wbi].rearrange("p (dc q) -> p dc q", q=512),
            wo.rearrange("(dc p) q -> p dc q", p=128)[:, :, jc * 512:(jc + 1) * 512]),
             writes=[f"wob{wbi}"], tag=f"3_lw{jc}", ch=f"wobl{wbi}")

    def load_oq(jc, qc):
        bi = qc % 2
        p.op("syd", lambda e, qc=qc, bi=bi: e.dma_start(
            OQ[bi][:], oh[qc].rearrange("hh p q -> p hh q")),
             reads=[f"oh:{qc}"], writes=[f"oq{bi}"], tag=f"3_ldo{jc}_{qc}")

    load_wo(0)
    if JCN > 1:
        load_wo(1)
    for jc in range(JCN):
        wbi = jc % 3
        if jc + 2 < JCN:
            load_wo(jc + 2)
        load_oq(jc, 0)
        for qci in range(QC):
            obi = qci % 2
            if qci + 1 < QC:
                load_oq(jc, qci + 1)
            for s in range(4):
                qt = qci * 4 + s
                pso = qt % 2
                psk = "P4" if pso == 0 else "P5"
                PS = P4 if pso == 0 else P5
                for hd in range(HEADS):
                    p.op("pe", lambda e, hd=hd, s=s, obi=obi, wbi=wbi, PS=PS: e.matmul(
                        PS[:], OQ[obi][:, hd, s * 128:(s + 1) * 128],
                        WOb[wbi][:, hd * 512:(hd + 1) * 512],
                        start=(hd == 0), stop=(hd == HEADS - 1)),
                         reads=[f"oq{obi}", f"wob{wbi}"], writes=[psk],
                         tag=f"3_mm{jc}_{qt}_{hd}")
                xb = qt % 2
                p.op("scd", lambda e, qt=qt, jc=jc, xb=xb: e.dma_start(
                    X1t[xb][:], x1h[qt * 128:(qt + 1) * 128, jc * 512:(jc + 1) * 512]),
                     reads=["x1h"], writes=[f"x1t{xb}"], tag=f"3_ldx1_{jc}_{qt}")
                p.op("dve", lambda e, xb=xb, PS=PS: e.tensor_tensor(
                    X2s[xb][:], X1t[xb][:], PS[:], AL.add),
                     reads=[f"x1t{xb}", psk], writes=[f"x2s{xb}"], tag=f"3_add{jc}_{qt}")
                p.op("syd", lambda e, qt=qt, jc=jc, xb=xb: e.dma_start(
                    x2h[qt * 128:(qt + 1) * 128, jc * 512:(jc + 1) * 512], X2s[xb][:]),
                     reads=[f"x2s{xb}"], writes=[f"x2h:{qt // 4}"], tag=f"3_stx2_{jc}_{qt}", ch=f"x2s{xb}")
    p.barrier()
    sb.reset(ph3)

    ph4 = sb.mark()
    Xg4 = [sb.alloc(f"X4_{i}", [128, 4, D], F32) for i in range(2)]
    XCS4 = [sb.alloc(f"XCS4_{i}", [128, 2, D], BF16) for i in range(2)]
    TRASH4 = sb.alloc("TRASH4", [128, D], BF16)
    STg4 = [sb.alloc(f"ST4{i}", [128, 16], F32) for i in range(2)]
    XTr4 = sb.alloc("XTr4", [128, 8, 512], BF16)
    MLT4 = sb.alloc("MLT4", [ERL, 512], F32)
    LTM4 = sb.alloc("LTM4", [128, 96], F32)
    GT8b = sb.alloc("GT8b", [128, 16], BF16)
    GT4 = sb.alloc("GT4", [4, 512], BF16)
    GM4 = sb.alloc("GM4", [ER, 512], BF16)
    x2r = x2h.rearrange("(gg s p) d -> gg p s d", s=4, p=128)
    h3r = h3h.rearrange("(gg s p) d -> gg p s d", s=4, p=128)
    outr = out.rearrange("(gg s p) d -> gg p s d", s=4, p=128)

    def load_group4(g):
        bi = g % 2
        for st in range(4):
            p.op("syd", lambda e, g=g, bi=bi, st=st: e.dma_start(
                Xg4[bi][:, st, :], x2r[g, :, st, :]),
                 reads=[f"x2h:{g}"], writes=[f"x4{bi}:{st}"], tag=f"4_ldx{g}_{st}",
                 ch=f"x4{bi}:{st}")

    def stage_a4(g):
        bi = g % 2
        emit_ln_cast(Xg4[bi], XCS4[bi], TRASH4, STg4[bi], f"x4{bi}", f"st4{bi}",
                     h3r[g], f"h3h:{g}", f"4_ln3g{g}", f"xcs4{bi}")

    def stage_b4(g):
        bi = g % 2
        X = Xg4[bi]
        xk = f"x4{bi}"
        emit_ffn(h3h, f"h3h:{g}", g * 512, X, xk, XTr4, MLT4, LTM4, GT8b, GT4, GM4, f"f2g{g}")
        p.op("syd", lambda e, g=g, X=X: e.dma_start(outr[g], X[:]),
             reads=[f"{xk}:{st}" for st in range(4)], tag=f"4_stout{g}", ch=f"{xk}:0")

    load_group4(0)
    if GO > 1:
        load_group4(1)
    stage_a4(0)
    for g in range(GO):
        if g + 1 < GO:
            stage_a4(g + 1)
        stage_b4(g)
        if g + 2 < GO:
            load_group4(g + 2)
        p.barrier()

    with nc.Block() as blk:
        p.materialize(blk)
    return nc, p


def _prep_host(x, wqn, wkn, wvn, won, A, B, Wr, cfg):
    At = np.ascontiguousarray(A.transpose(2, 0, 1).reshape(cfg["D"], 64))
    WrT = np.ascontiguousarray(Wr.T)
    AWr = np.ascontiguousarray(np.concatenate([At, WrT], axis=1))
    Bf = np.ascontiguousarray(B.transpose(0, 2, 1).reshape(64, cfg["D"]))
    return AWr, Bf


def kernel(L0_attn_x, L0_attn_wq, L0_attn_wk, L0_attn_wv, L0_attn_wo,
           L0_ffn_A, L0_ffn_B, L0_ffn_Wr):
    x = np.ascontiguousarray(np.asarray(L0_attn_x, dtype=np.float32))
    wqn = np.ascontiguousarray(np.asarray(L0_attn_wq, dtype=np.float32))
    wkn = np.ascontiguousarray(np.asarray(L0_attn_wk, dtype=np.float32))
    wvn = np.ascontiguousarray(np.asarray(L0_attn_wv, dtype=np.float32))
    won = np.ascontiguousarray(np.asarray(L0_attn_wo, dtype=np.float32))
    A = np.asarray(L0_ffn_A, dtype=np.float32)
    B = np.asarray(L0_ffn_B, dtype=np.float32)
    Wr = np.asarray(L0_ffn_Wr, dtype=np.float32)
    AWr, Bf = _prep_host(x, wqn, wkn, wvn, won, A, B, Wr, CFG_FULL)

    nc, _ = build_program(CFG_FULL)
    in_maps = []
    for c in range(8):
        b, h = c // 2, c % 2
        xb = x[b, h * 2048:(h + 1) * 2048]
        in_maps.append({"x0": np.ascontiguousarray(xb), "wq": wqn, "wk": wkn,
                        "wv": wvn, "wo": won, "awr": AWr, "bfl": Bf})
    res = run_bass_kernel_spmd(nc, in_maps, core_ids=list(range(8)))
    outp = np.empty((4, 4096, 4096), dtype=np.float32)
    for c in range(8):
        b, h = c // 2, c % 2
        outp[b, h * 2048:(h + 1) * 2048] = res.results[c]["out"]
    return outp
